# version 02
# baseline (speedup 1.0000x reference)
import types
import numpy as np

import concourse.bass as bass
import concourse.mybir as mybir
import concourse.library_config as libcfg
from concourse.bass_isa import ReduceOp
from concourse.bass_utils import run_bass_kernel_spmd

F32 = mybir.dt.float32
BF16 = mybir.dt.bfloat16
AF = mybir.ActivationFunctionType
ALU = mybir.AluOpType
AX = mybir.AxisListType
AP = bass.AP

N_CORES = 8
T = 4096
D = 1024
NCH = 8
H = 16
NPAIR = 8
CK = 64
LT = 256
NTILE = T // LT
CPT = LT // CK

N_STAGES = 4
L0F_PARTS = 5
DBG_DUMP = None


class DbgStop(Exception):
    pass


def host_consts():
    c = {}
    c["c_id"] = np.eye(128, dtype=np.float32)
    oc = np.zeros((128, 6), dtype=np.float32)
    oc[:, 0] = 1.0
    oc[:, 1] = 1.0 / D
    oc[:, 2] = 1e-5
    oc[:, 3] = 1e-6
    oc[:, 4] = -30.0
    oc[:, 5] = 1e-12
    c["c_onescols"] = oc
    ones2 = np.zeros((128, 2), dtype=np.float32)
    ones2[0:64, 0] = 1.0
    ones2[64:128, 1] = 1.0
    c["c_ones2"] = ones2
    c["c_ones1x"] = np.ones((1, 128), dtype=np.float32)
    sel2 = np.zeros((2, 128), dtype=np.float32)
    sel2[0, 0:64] = 1.0
    sel2[1, 64:128] = 1.0
    c["c_sel2"] = sel2
    c["c_sello"] = np.ascontiguousarray(sel2[0:1])
    c["c_selhi"] = np.ascontiguousarray(sel2[1:2])
    up = np.triu(np.ones((64, 64), dtype=np.float32), 1)
    lo = np.tril(np.ones((64, 64), dtype=np.float32), -1)
    tri = np.triu(np.ones((64, 64), dtype=np.float32), 0)
    c["c_negup"] = np.ascontiguousarray(-np.concatenate([up, up], 0))
    c["c_neglo"] = np.ascontiguousarray(-np.concatenate([lo, lo], 0))
    c["c_triuincl"] = np.ascontiguousarray(np.concatenate([tri, tri], 0))
    rot = np.zeros((64, 64), dtype=np.float32)
    for m in range(32):
        rot[m + 32, m] = -1.0
        rot[m, m + 32] = 1.0
    rp = np.zeros((128, 128), dtype=np.float32)
    rp[0:64, 0:64] = rot
    rp[64:128, 64:128] = rot
    c["c_rot"] = rp
    r_ = np.arange(128)
    mcur = np.where(r_[None, :] < r_[:, None], -1e30, 0.0).astype(np.float32)
    mprev = np.where(r_[None, :] > r_[:, None], -1e30, 0.0).astype(np.float32)
    c["c_mcur"] = mcur
    c["c_mprev"] = mprev
    hi = np.arange(0, 64, 2, dtype=np.float64)
    inv_freq = 1.0 / 10000.0 ** (hi / 64.0)
    tt = np.arange(T, dtype=np.float64)
    freqs = np.concatenate([tt[:, None] * inv_freq[None, :]] * 2, axis=1)
    base = (hi + 0.4 * 64.0) / (1.4 * 64.0)
    power = (tt - T // 2) / 256.0
    scale = np.concatenate([base[None, :] ** power[:, None]] * 2, axis=1)
    cs = (np.cos(freqs) * scale).T
    ss = (np.sin(freqs) * scale).T
    csr = (np.cos(freqs) / scale).T
    ssr = (np.sin(freqs) / scale).T
    dup = lambda m: np.ascontiguousarray(np.concatenate([m, m], 0).astype(np.float32))
    c["c_cs"] = dup(cs)
    c["c_ss"] = dup(ss)
    c["c_csr"] = dup(csr)
    c["c_ssr"] = dup(ssr)
    return c


class Ctr:
    def __init__(self):
        self.v = {}

    def add(self, sem, n=1):
        k = id(sem)
        self.v[k] = self.v.get(k, 0) + n
        return self.v[k]

    def get(self, sem):
        return self.v.get(id(sem), 0)


class B:
    def __init__(self, nc, es):
        self.nc = nc
        self.es = es
        self.C = Ctr()
        self.sem_w = es.enter_context(nc.semaphore("sem_w"))
        self.sem_g = es.enter_context(nc.semaphore("sem_g"))
        self.sem_t = es.enter_context(nc.semaphore("sem_t"))
        self.sem_v = es.enter_context(nc.semaphore("sem_v"))
        self.sem_s = es.enter_context(nc.semaphore("sem_s"))
        self.sem_p = es.enter_context(nc.semaphore("sem_p"))
        self.cst = {}
        self.hb = {}
        self.sg = {}

    def sb(self, name, shape, dtype=F32):
        return self.nc.alloc_sbuf_tensor(name, list(shape), dtype)

    def psum(self, name, shape, dtype=F32):
        return self.es.enter_context(self.nc.psum_tensor(name, list(shape), dtype))

    def dma(self, eng, dst, src, sem=None):
        sem = self.sem_w if sem is None else sem
        eng.dma_start(dst, src).then_inc(sem, 16)
        self.C.add(sem, 16)
        eng.wait_ge(sem, self.C.get(sem))

    def wdma(self, eng, sem=None):
        sem = self.sem_w if sem is None else sem
        eng.wait_ge(sem, self.C.get(sem))

    def tick(self, eng, sem):
        eng.drain().then_inc(sem, 1)
        self.C.add(sem, 1)

    def wt(self, eng):
        eng.wait_ge(self.sem_t, self.C.get(self.sem_t))

    def wv(self, eng):
        eng.wait_ge(self.sem_v, self.C.get(self.sem_v))

    def ws(self, eng):
        eng.wait_ge(self.sem_s, self.C.get(self.sem_s))

    def mmtick(self, inst, n=1):
        inst.then_inc(self.sem_t, n)
        self.C.add(self.sem_t, n)


class SrowsView:

    def __init__(self, pa, pb):
        self.banks = (pa, pb)

    def __getitem__(self, idx):
        _, cs = idx
        a = cs.start or 0
        bnd = cs.stop
        assert a // 512 == (bnd - 1) // 512, (a, bnd)
        t = self.banks[a // 512]
        off = a % 512
        return t[0:16, off:off + (bnd - a)]


def wslab_src(src, j0, ncols):
    a = src.ap() if callable(getattr(src, "ap", None)) else src
    return a[:, j0:j0 + ncols].rearrange("(c p) n -> p c n", p=128)


def slab3(buf, off, nch, ncols):
    return buf[:, off:off + nch * ncols].rearrange("p (c n) -> p c n", n=ncols)


def pairbc(col8):
    return col8.unsqueeze(2).broadcast_to([128, 8, 64])


def pair3(t512):
    return t512.rearrange("p (a b) -> p a b", b=64)


def maskbc(m):
    return m.unsqueeze(1).broadcast_to([128, 8, 64])


def tpose64(t, out_ps, src, h, cid):
    if h == 0:
        return t.transpose(out_ps, src, cid[0:64, 0:64])
    return t.matmul(out_ps, src, cid[64:128, 64:128], start=True, stop=True,
                    tile_position=(64, 64))



def stage_l0_attn(b, xf):
    nc = b.nc
    hb = b.hb
    cst = b.cst
    z = types.SimpleNamespace()
    pool = b.pool
    cur = [0]

    def carve(units):
        o = cur[0]
        cur[0] += units * 512
        return pool[:, o:o + units * 512]

    z.xn = carve(4)
    z.qcf = carve(4)
    z.kcf = carve(4)
    z.vcf = carve(4)
    z.of = carve(4)
    z.wslab = carve(4)
    z.gdl = carve(2)
    z.ktm = carve(1)
    z.qtm = carve(1)
    z.kfm = carve(1)
    z.qfm = carve(1)
    z.ntmat = carve(1)
    z.nx = carve(1)
    z.npows4 = carve(1)
    z.smat = carve(1)
    sm = carve(1)
    z.marg = sm[:, 0:72]
    z.wa = sm[:, 72:200]
    z.wb = sm[:, 200:328]
    z.cvw = sm[:, 328:424]
    z.cvb = sm[:, 424:448]
    z.scc = sm[:, 448:480]
    z.nrm = sm[:, 480:512]
    rab = b.sb("a_rab", [128, 1034])
    z.rmsc = rab[:, 0:8]
    z.abias = rab[0:16, 1032:1034]
    b.nrow_a = rab[0:4, 8:520]
    b.nrow_b = rab[0:4, 520:1032]
    z.srows = SrowsView(b.pss[6], b.pss[7])
    z.g2t = z.qtm
    z.nmat = z.nx
    z.npw = z.nx
    z.xsol = z.nx
    z.id16 = cst["c_id"][0:16, 0:16]

    def npowT(lvl):
        return z.npows4[:] if lvl == 4 else z.xn[:, lvl * 512:(lvl + 1) * 512]

    z.npowT = npowT

    with nc.Block() as blk:
        @blk.sync
        def _(sy):
            for cc in range(NCH):
                b.dma(sy, z.wa[:, cc * 16:(cc + 1) * 16],
                      hb["L0_attn_Wa"].ap()[cc * 128:(cc + 1) * 128, :])
                b.dma(sy, z.wb[:, cc * 16:(cc + 1) * 16],
                      hb["L0_attn_Wb"].ap()[cc * 128:(cc + 1) * 128, :])
            b.dma(sy, z.cvw[:, 0:32], hb["L0_attn_qcwP"].ap())
            b.dma(sy, z.cvw[:, 32:64], hb["L0_attn_kcwP"].ap())
            b.dma(sy, z.cvw[:, 64:96], hb["L0_attn_vcwP"].ap())
            b.dma(sy, z.cvb[:, 0:8], hb["L0_attn_qcbP"].ap())
            b.dma(sy, z.cvb[:, 8:16], hb["L0_attn_kcbP"].ap())
            b.dma(sy, z.cvb[:, 16:24], hb["L0_attn_vcbP"].ap())
            b.dma(sy, z.abias[:], hb["L0_attn_abP"].ap())
            b.dma(sy, z.rmsc[:], hb["L0_attn_rms_wP"].ap())
            b.wdma(sy)

        @blk.vector
        def _(v):
            v.memset(z.smat[:], 0.0)
            v.memset(z.marg[:], 0.0)

    z.ps_bc0 = b.pss[4][:, 0:256]
    z.ps_bc1 = b.pss[5][:, 0:256]
    z.ps_p = b.pss[0][:, :]
    z.ps_a = b.pss[1][:, :]
    z.ps_b = b.pss[2][:, :]
    z.ps_o = b.pss[3][:, :]

    for it in range(NTILE):
        _l0a_tile_proj(b, xf, it, z)
        _l0a_tile_rows(b, it, z)
        for c in range(CPT):
            _l0a_chunk(b, it, c, z)
        _l0a_tile_out(b, xf, it, z)


def _l0a_tile_proj(b, xf, it, z):
    nc = b.nc
    C = b.C
    cst = b.cst
    hb = b.hb
    t0 = it * LT
    AL, BE, A_, VB = range(4)
    mu_r = b.nrow_a[0:1, 0:LT]
    rstd_r = b.nrow_b[0:1, 0:LT]
    var_r = b.nrow_b[0:1, 0:LT]

    with nc.Block() as blk:
        @blk.scalar
        def _(s):
            for cc in range(NCH):
                s.activation(z.xn[:, cc * LT:(cc + 1) * LT],
                             xf[:, cc * T + t0:cc * T + t0 + LT], AF.Square)
            b.tick(s, b.sem_s)

        @blk.tensor
        def _(t):
            b.ws(t)
            for cc in range(NCH):
                t.matmul(z.ps_p[0:1, 0:LT], cst["c_onescols"][:, 1:2],
                         xf[:, cc * T + t0:cc * T + t0 + LT],
                         start=(cc == 0), stop=(cc == NCH - 1))
            ins = None
            for cc in range(NCH):
                ins = t.matmul(z.ps_p[0:1, LT:2 * LT], cst["c_onescols"][:, 1:2],
                               z.xn[:, cc * LT:(cc + 1) * LT],
                               start=(cc == 0), stop=(cc == NCH - 1))
            b.mmtick(ins)

        @blk.vector
        def _(v):
            b.wt(v)
            v.tensor_copy(mu_r, z.ps_p[0:1, 0:LT])
            v.drain()
            v.tensor_mul(var_r, mu_r, mu_r)
            v.drain()
            v.tensor_sub(var_r, z.ps_p[0:1, LT:2 * LT], var_r)
            b.tick(v, b.sem_v)

        @blk.scalar
        def _(s):
            b.wv(s)
            s.activation(rstd_r, var_r, AF.Sqrt, bias=cst["c_onescols"][0:1, 2:3])
            b.tick(s, b.sem_s)

        @blk.vector
        def _(v):
            b.ws(v)
            v.reciprocal(rstd_r, rstd_r)
            b.tick(v, b.sem_v)

        @blk.tensor
        def _(t):
            b.wv(t)
            t.matmul(z.ps_bc0[:], cst["c_ones1x"][0:1, :], mu_r, start=True, stop=True)
            ins = t.matmul(z.ps_bc1[:], cst["c_ones1x"][0:1, :], rstd_r, start=True, stop=True)
            b.mmtick(ins)

        @blk.vector
        def _(v):
            b.wt(v)
            for cc in range(NCH):
                v.tensor_sub(z.xn[:, cc * LT:(cc + 1) * LT],
                             xf[:, cc * T + t0:cc * T + t0 + LT], z.ps_bc0[:])
            v.drain()
            for cc in range(NCH):
                v.tensor_mul(z.xn[:, cc * LT:(cc + 1) * LT],
                             z.xn[:, cc * LT:(cc + 1) * LT], z.ps_bc1[:])
            b.tick(v, b.sem_v)

        @blk.tensor
        def _(t):
            b.wv(t)
            for cc in range(NCH):
                t.matmul(z.ps_p[0:16, 0:LT], z.wa[:, cc * 16:(cc + 1) * 16],
                         z.xn[:, cc * LT:(cc + 1) * LT], start=(cc == 0), stop=(cc == NCH - 1))
            ins = None
            for cc in range(NCH):
                ins = t.matmul(z.ps_p[0:16, LT:2 * LT], z.wb[:, cc * 16:(cc + 1) * 16],
                               z.xn[:, cc * LT:(cc + 1) * LT], start=(cc == 0), stop=(cc == NCH - 1))
            b.mmtick(ins)

        @blk.scalar
        def _(s):
            b.wt(s)
            s.activation(z.srows[:, AL * LT:(AL + 1) * LT], z.ps_p[0:16, 0:LT],
                         AF.Sigmoid, bias=z.abias[0:16, 0:1])
            s.activation(z.srows[:, BE * LT:(BE + 1) * LT], z.ps_p[0:16, LT:2 * LT],
                         AF.Sigmoid, bias=z.abias[0:16, 1:2])
            b.tick(s, b.sem_s)

        @blk.vector
        def _(v):
            b.ws(v)
            v.tensor_scalar(z.srows[:, AL * LT:(AL + 1) * LT],
                            z.srows[:, AL * LT:(AL + 1) * LT],
                            0.1, 1.0, ALU.max, ALU.min)
            b.tick(v, b.sem_v)

        specs = []
        for j in range(NCH):
            specs.append(("L0_attn_Wq", j, z.qcf, 0, 0, True))
        for j in range(NCH):
            specs.append(("L0_attn_Wk", j, z.kcf, 3, 32, True))
        for j in range(NCH):
            specs.append(("L0_attn_Wv", j, z.vcf, 6, 64, True))
        for j in range(4):
            specs.append(("L0_attn_Wgd", j, z.gdl, 0, 0, False))

        for sidx, (wname, j, dst, mgo, cwo, isconv) in enumerate(specs):
            bufp = sidx % 2

            @blk.sync
            def _(sy, wname=wname, j=j, bufp=bufp, sidx=sidx):
                if sidx >= 2:
                    sy.wait_ge(b.sem_t, C.get(b.sem_t) - 1)
                b.dma(sy, slab3(z.wslab, bufp * NCH * 128, NCH, 128),
                      wslab_src(hb[wname], j * 128, 128))

            @blk.tensor
            def _(t, bufp=bufp):
                b.wdma(t)
                b.wv(t)
                ins = None
                for cc in range(NCH):
                    ins = t.matmul(
                        z.ps_p[:, 0:LT],
                        z.wslab[:, bufp * NCH * 128 + cc * 128:bufp * NCH * 128 + (cc + 1) * 128],
                        z.xn[:, cc * LT:(cc + 1) * LT],
                        start=(cc == 0), stop=(cc == NCH - 1))
                b.mmtick(ins)

            @blk.vector
            def _(v, j=j, dst=dst, mgo=mgo, cwo=cwo, isconv=isconv):
                b.wt(v)
                o = j * LT
                if not isconv:
                    v.tensor_copy(dst[:, o:o + LT], z.ps_p[:, 0:LT])
                else:
                    wc = lambda k: z.cvw[:, cwo + j * 4 + k:cwo + j * 4 + k + 1]
                    mg = z.marg[:, j * 9 + mgo:j * 9 + mgo + 3]
                    v.tensor_scalar(dst[:, o:o + LT], z.ps_p[:, 0:LT], wc(3), None, ALU.mult)
                    v.drain()
                    for k in range(3):
                        sh = 3 - k
                        v.scalar_tensor_tensor(dst[:, o + sh:o + LT], z.ps_p[:, 0:LT - sh],
                                               wc(k), dst[:, o + sh:o + LT], ALU.mult, ALU.add)
                        v.drain()
                        v.scalar_tensor_tensor(dst[:, o:o + sh], mg[:, k:k + sh],
                                               wc(k), dst[:, o:o + sh], ALU.mult, ALU.add)
                        v.drain()
                    v.tensor_copy(mg[:], z.ps_p[:, LT - 3:LT])
                b.tick(v, b.sem_v)

        @blk.scalar
        def _(s):
            b.wv(s)
            for dst, cbo in ((z.qcf, 0), (z.kcf, 8), (z.vcf, 16)):
                for j in range(NCH):
                    s.activation(dst[:, j * LT:(j + 1) * LT], dst[:, j * LT:(j + 1) * LT],
                                 AF.Silu, bias=z.cvb[:, cbo + j:cbo + j + 1])
            b.tick(s, b.sem_s)


def _l0a_tile_rows(b, it, z):
    nc = b.nc
    AL, BE, A_, VB = range(4)
    sr = lambda k, lo=0, hi=LT: z.srows[:, k * LT + lo:k * LT + hi]

    with nc.Block() as blk:
        @blk.vector
        def _(v):
            dummy = b.cst["c_id"][0:16, 0:CK]
            sc1 = z.wslab[0:16, 0:LT]
            sc2 = z.wslab[0:16, LT:2 * LT]
            for c in range(CPT):
                v.tensor_tensor_scan(sr(A_, c * CK, (c + 1) * CK), sr(AL, c * CK, (c + 1) * CK),
                                     dummy, 1.0, ALU.mult, ALU.bypass)
            v.drain()
            v.reciprocal(sc1, sr(AL))
            v.tensor_copy(sc2, sr(AL))
            v.drain()
            v.tensor_mul(sr(VB), sr(BE), sc1)
            v.drain()
            v.tensor_mul(sr(BE), sr(BE), sc2)
            b.tick(v, b.sem_v)

        @blk.scalar
        def _(s):
            b.wv(s)
            s.activation(sr(VB), sr(VB), AF.Sqrt)
            s.activation(sr(BE), sr(BE), AF.Sqrt)
            b.tick(s, b.sem_s)

        @blk.vector
        def _(v):
            b.ws(v)
            sc1 = z.wslab[0:16, 0:LT]
            v.reciprocal(sc1, sr(A_))
            v.drain()
            v.tensor_mul(sr(BE), sr(BE), sc1)
            v.drain()
            for c in range(CPT):
                v.tensor_copy(sr(AL, c * CK, (c + 1) * CK),
                              sr(A_, (c + 1) * CK - 1, (c + 1) * CK).broadcast_to([16, CK]))
            b.tick(v, b.sem_v)


def _l0a_chunk(b, it, c, z):
    nc = b.nc
    cst = b.cst
    AL, BE, A_, VB = range(4)
    R2, VBS, AH, AC = VB, BE, A_, AL
    id64 = cst["c_id"][0:64, 0:64]
    PR = lambda h: slice(64 * h, 64 * h + 64)
    CB = lambda p: slice(64 * p, 64 * p + 64)

    def fmsl(buf, p, h):
        return buf[PR(h), p * LT + c * CK:p * LT + (c + 1) * CK]

    with nc.Block() as blk:
        @blk.tensor
        def _(t):
            b.wv(t)
            b.ws(t)
            ins = None
            for src, ps in ((z.qcf, z.ps_a), (z.kcf, z.ps_b)):
                for p in range(NPAIR):
                    for h in range(2):
                        ins = tpose64(t, ps[PR(h), CB(p)], fmsl(src, p, h), h, cst["c_id"])
            b.mmtick(ins)

        @blk.vector
        def _(v):
            b.wt(v)
            v.tensor_copy(z.qtm[:], z.ps_a[:])
            v.tensor_copy(z.ktm[:], z.ps_b[:])
            v.drain()
            v.tensor_mul(z.nx[:], z.qtm[:], z.qtm[:])
            v.drain()
            v.tensor_reduce(z.nrm[:, 0:8], pair3(z.nx[:]), AX.X, ALU.add)
            v.drain()
            v.tensor_mul(z.nx[:], z.ktm[:], z.ktm[:])
            v.drain()
            v.tensor_reduce(z.nrm[:, 8:16], pair3(z.nx[:]), AX.X, ALU.add)
            b.tick(v, b.sem_v)

        @blk.scalar
        def _(s):
            b.wv(s)
            s.activation(z.nrm[:, 16:24], z.nrm[:, 0:8], AF.Sqrt)
            s.activation(z.nrm[:, 24:32], z.nrm[:, 8:16], AF.Sqrt)
            b.tick(s, b.sem_s)

        @blk.vector
        def _(v):
            b.ws(v)
            v.tensor_scalar(z.nrm[:, 16:32], z.nrm[:, 16:32], 1e-6, None, ALU.max)
            v.drain()
            v.reciprocal(z.nrm[:, 16:32], z.nrm[:, 16:32])
            b.tick(v, b.sem_v)

        stg = z.wslab[0:16, 0:512]

        @blk.vector
        def _(v):
            b.wv(v)
            for gi, blkid in enumerate((R2, VBS, AH, AC)):
                sl = z.srows[:, blkid * LT + c * CK:blkid * LT + (c + 1) * CK]
                v.tensor_copy(stg[:, gi * 128:gi * 128 + 64], sl)
                v.tensor_copy(stg[:, gi * 128 + 64:gi * 128 + 128], sl)
            b.tick(v, b.sem_v)

        @blk.tensor
        def _(t):
            b.wv(t)
            ins = None
            for gi in range(4):
                ins = t.transpose(z.ps_bc0[:, gi * 16:(gi + 1) * 16],
                                  stg[:, gi * 128:(gi + 1) * 128], z.id16)
            b.mmtick(ins)

        @blk.vector
        def _(v):
            b.wt(v)
            for gi in range(4):
                v.tensor_copy(z.scc[0:64, gi * 8:(gi + 1) * 8],
                              z.ps_bc0[0:64, gi * 16:(gi + 1) * 16:2])
                v.tensor_copy(z.scc[64:128, gi * 8:(gi + 1) * 8],
                              z.ps_bc0[64:128, gi * 16 + 1:(gi + 1) * 16:2])
            v.drain()
            v.tensor_mul(z.scc[:, 0:8], z.scc[:, 0:8], z.nrm[:, 24:32])
            v.tensor_mul(z.scc[:, 16:24], z.scc[:, 16:24], z.nrm[:, 16:24])
            v.drain()
            v.tensor_tensor(pair3(z.ktm[:]), pair3(z.ktm[:]), pairbc(z.scc[:, 0:8]), ALU.mult)
            v.tensor_tensor(pair3(z.qtm[:]), pair3(z.qtm[:]), pairbc(z.scc[:, 16:24]), ALU.mult)
            b.tick(v, b.sem_v)

        @blk.tensor
        def _(t):
            b.wv(t)
            ins = None
            for src, ps in ((z.ktm, z.ps_a), (z.qtm, z.ps_b)):
                for p in range(NPAIR):
                    for h in range(2):
                        ins = tpose64(t, ps[PR(h), CB(p)], src[PR(h), CB(p)], h, cst["c_id"])
            b.mmtick(ins)

        @blk.vector
        def _(v):
            b.wt(v)
            v.tensor_copy(z.kfm[:], z.ps_a[:])
            v.tensor_copy(z.qfm[:], z.ps_b[:])
            b.tick(v, b.sem_v)

        @blk.tensor
        def _(t):
            b.wv(t)
            ins = None
            for p in range(NPAIR):
                for h in range(2):
                    t.matmul(z.ps_o[PR(h), CB(p)], z.kfm[PR(h), CB(p)], z.kfm[PR(h), CB(p)],
                             start=True, stop=True, tile_position=(64 * h, 64 * h))
                    ins = t.matmul(z.ps_a[PR(h), CB(p)], z.kfm[PR(h), CB(p)], z.qfm[PR(h), CB(p)],
                                   start=True, stop=True, tile_position=(64 * h, 64 * h))
            b.mmtick(ins)

        @blk.vector
        def _(v):
            b.wt(v)
            v.tensor_tensor(pair3(z.ntmat[:]), pair3(z.ps_o[:]), maskbc(cst["c_negup"]), ALU.mult)
            v.tensor_tensor(pair3(z.g2t[:]), pair3(z.ps_a[:]), maskbc(cst["c_triuincl"]), ALU.mult)
            v.drain()
            v.tensor_tensor(pair3(z.nmat[:]), pair3(z.ps_o[:]), maskbc(cst["c_neglo"]), ALU.mult)
            b.tick(v, b.sem_v)

        prevp, prevt = z.nmat, z.ntmat
        for lvl in range(5):
            last = lvl == 4

            @blk.tensor
            def _(t, prevp=prevp, prevt=prevt, last=last, lvl=lvl):
                if lvl == 0:
                    b.wv(t)
                else:
                    b.ws(t)
                ins = None
                for p in range(NPAIR):
                    for h in range(2):
                        ins = t.matmul(z.ps_a[PR(h), CB(p)], prevp[PR(h), CB(p)],
                                       prevt[PR(h), CB(p)], start=True, stop=True,
                                       tile_position=(64 * h, 64 * h))
                        if not last:
                            ins = t.matmul(z.ps_o[PR(h), CB(p)], prevt[PR(h), CB(p)],
                                           prevp[PR(h), CB(p)], start=True, stop=True,
                                           tile_position=(64 * h, 64 * h))
                b.mmtick(ins)

            @blk.scalar
            def _(s, lvl=lvl, last=last):
                b.wt(s)
                s.activation(z.npowT(lvl), z.ps_a[:], AF.Identity)
                if not last:
                    s.activation(z.npw[:], z.ps_o[:], AF.Identity)
                b.tick(s, b.sem_s)

            prevp = z.npw
            prevt = z.npowT(lvl)

        @blk.tensor
        def _(t):
            b.ws(t)
            ins = None
            for p in range(NPAIR):
                for h in range(2):
                    ins = tpose64(t, z.ps_b[PR(h), CB(p)], fmsl(z.vcf, p, h), h, cst["c_id"])
            for p in range(NPAIR):
                for h in range(2):
                    ins = t.matmul(z.ps_o[PR(h), CB(p)], z.kfm[PR(h), CB(p)],
                                   z.smat[PR(h), CB(p)], start=True, stop=True,
                                   tile_position=(64 * h, 64 * h))
            b.mmtick(ins)

        @blk.vector
        def _(v):
            b.wt(v)
            v.tensor_tensor(pair3(z.xsol[:]), pair3(z.ps_b[:]), pairbc(z.scc[:, 8:16]), ALU.mult)
            v.drain()
            v.tensor_sub(z.xsol[:], z.xsol[:], z.ps_o[:])
            b.tick(v, b.sem_v)

        for j in range(6):
            @blk.tensor
            def _(t, j=j):
                b.wv(t)
                lhs = z.ntmat[:] if j == 0 else z.npowT(j - 1)
                ins = None
                for p in range(NPAIR):
                    for h in range(2):
                        ins = t.matmul(z.ps_a[PR(h), CB(p)], lhs[PR(h), CB(p)],
                                       z.xsol[PR(h), CB(p)], start=True, stop=True,
                                       tile_position=(64 * h, 64 * h))
                b.mmtick(ins)

            @blk.vector
            def _(v):
                b.wt(v)
                v.tensor_add(z.xsol[:], z.xsol[:], z.ps_a[:])
                b.tick(v, b.sem_v)

        @blk.tensor
        def _(t):
            b.wv(t)
            ins = None
            for p in range(NPAIR):
                for h in range(2):
                    t.matmul(z.ps_o[PR(h), CB(p)], z.smat[PR(h), CB(p)], z.qfm[PR(h), CB(p)],
                             start=True, stop=False, tile_position=(64 * h, 64 * h))
                    ins = t.matmul(z.ps_o[PR(h), CB(p)], z.xsol[PR(h), CB(p)], z.g2t[PR(h), CB(p)],
                                   start=False, stop=True, tile_position=(64 * h, 64 * h))
            for p in range(NPAIR):
                for h in range(2):
                    ins = t.matmul(z.ps_a[PR(h), CB(p)], z.ktm[PR(h), CB(p)],
                                   z.xsol[PR(h), CB(p)], start=True, stop=True,
                                   tile_position=(64 * h, 64 * h))
            b.mmtick(ins)

        @blk.vector
        def _(v):
            b.wt(v)
            v.tensor_copy(
                z.of[:].rearrange("p (a b) -> p a b", b=LT)[:, :, c * CK:(c + 1) * CK],
                pair3(z.ps_o[:]))
            v.tensor_add(z.smat[:], z.smat[:], z.ps_a[:])
            v.drain()
            v.tensor_tensor(pair3(z.smat[:]), pair3(z.smat[:]), pairbc(z.scc[:, 24:32]), ALU.mult)
            b.tick(v, b.sem_v)


def _l0a_tile_out(b, xf, it, z):
    nc = b.nc
    C = b.C
    cst = b.cst
    hb = b.hb
    t0 = it * LT
    crows = b.nrow_a[0:2, 0:LT]
    gf = z.qcf

    with nc.Block() as blk:
        @blk.scalar
        def _(s):
            for j4 in range(4):
                s.activation(z.gdl[:, j4 * LT:(j4 + 1) * LT], z.gdl[:, j4 * LT:(j4 + 1) * LT],
                             AF.Silu)
            b.tick(s, b.sem_s)

        for j in range(NCH):
            bufp = j % 2
            ps = z.ps_bc0 if j % 2 == 0 else z.ps_bc1

            @blk.sync
            def _(sy, j=j, bufp=bufp):
                if j >= 2:
                    sy.wait_ge(b.sem_t, C.get(b.sem_t) - 1)
                b.dma(sy, slab3(z.wslab, bufp * NCH * 128, 4, 128),
                      wslab_src(hb["L0_attn_Wgu"], j * 128, 128))

            @blk.tensor
            def _(t, bufp=bufp, ps=ps):
                b.wdma(t)
                b.ws(t)
                ins = None
                for c4 in range(4):
                    ins = t.matmul(ps[:, 0:LT],
                                   z.wslab[:, bufp * NCH * 128 + c4 * 128:bufp * NCH * 128 + (c4 + 1) * 128],
                                   z.gdl[:, c4 * LT:(c4 + 1) * LT],
                                   start=(c4 == 0), stop=(c4 == 3))
                b.mmtick(ins)

            @blk.scalar
            def _(s, j=j, ps=ps):
                b.wt(s)
                s.activation(gf[:, j * LT:(j + 1) * LT], ps[:, 0:LT], AF.Sigmoid)
                b.tick(s, b.sem_s)

    with nc.Block() as blk:
        @blk.scalar
        def _(s):
            for p in range(NPAIR):
                s.activation(z.xn[:, p * LT:(p + 1) * LT], z.of[:, p * LT:(p + 1) * LT],
                             AF.Square)
            b.tick(s, b.sem_s)

        for p in range(NPAIR):
            psb = z.ps_bc0 if p % 2 == 0 else z.ps_bc1

            @blk.tensor
            def _(t, p=p):
                if p == 0:
                    b.ws(t)
                else:
                    b.wv(t)
                ins = t.matmul(z.ps_p[0:2, 0:LT], cst["c_ones2"],
                               z.xn[:, p * LT:(p + 1) * LT], start=True, stop=True)
                b.mmtick(ins)

            @blk.scalar
            def _(s):
                b.wt(s)
                s.activation(crows[0:2, :], z.ps_p[0:2, 0:LT], AF.Sqrt,
                             bias=cst["c_onescols"][0:2, 3:4], scale=1.0 / 64.0)
                b.tick(s, b.sem_s)

            @blk.vector
            def _(v):
                b.ws(v)
                v.reciprocal(crows[0:2, :], crows[0:2, :])
                b.tick(v, b.sem_v)

            @blk.tensor
            def _(t, psb=psb):
                b.wv(t)
                ins = t.matmul(psb[:, 0:LT], cst["c_sel2"], crows[0:2, :],
                               start=True, stop=True)
                b.mmtick(ins)

            @blk.vector
            def _(v, p=p, psb=psb):
                b.wt(v)
                v.tensor_mul(z.of[:, p * LT:(p + 1) * LT], z.of[:, p * LT:(p + 1) * LT],
                             psb[:, 0:LT])
                v.drain()
                v.tensor_scalar(z.of[:, p * LT:(p + 1) * LT], z.of[:, p * LT:(p + 1) * LT],
                                z.rmsc[:, p:p + 1], None, ALU.mult)
                v.drain()
                v.tensor_mul(z.of[:, p * LT:(p + 1) * LT], z.of[:, p * LT:(p + 1) * LT],
                             gf[:, p * LT:(p + 1) * LT])
                b.tick(v, b.sem_v)

    with nc.Block() as blk:
        for j in range(NCH):
            bufp = j % 2

            @blk.sync
            def _(sy, j=j, bufp=bufp):
                if j >= 2:
                    sy.wait_ge(b.sem_t, C.get(b.sem_t) - 1)
                b.dma(sy, slab3(z.wslab, bufp * NCH * 128, NCH, 128),
                      wslab_src(hb["L0_attn_Wo"], j * 128, 128))

            @blk.tensor
            def _(t, bufp=bufp, j=j):
                b.wdma(t)
                if j > 0:
                    b.wv(t)
                ins = None
                for p in range(NPAIR):
                    ins = t.matmul(
                        z.ps_p[:, 0:LT],
                        z.wslab[:, bufp * NCH * 128 + p * 128:bufp * NCH * 128 + (p + 1) * 128],
                        z.of[:, p * LT:(p + 1) * LT],
                        start=(p == 0), stop=(p == NPAIR - 1))
                b.mmtick(ins)

            @blk.vector
            def _(v, j=j):
                b.wt(v)
                v.tensor_add(xf[:, j * T + t0:j * T + t0 + LT],
                             xf[:, j * T + t0:j * T + t0 + LT], z.ps_p[:, 0:LT])
                b.tick(v, b.sem_v)



def stage_l0_ffn(b, xf):
    nc = b.nc
    hb = b.hb
    cst = b.cst
    z = types.SimpleNamespace()
    pool = b.pool
    z.ps_s0 = b.pss[0][:, :]
    z.ps_s1 = b.pss[1][:, :]
    z.ps_o = b.pss[2][:, :]
    z.ps_d = b.pss[3][:, :]
    z.ps_v0 = b.pss[4][:, :]
    z.ps_v1 = b.pss[5][:, :]
    cur = [0]

    def carve(units):
        o = cur[0]
        cur[0] += units * 512
        return pool[:, o:o + units * 512]

    z.qb = carve(4)
    z.kb = carve(8)
    z.vt = carve(9)
    z.ob = carve(4)
    z.wsl = carve(4)
    z.rt = carve(2)
    z.at = carve(1)
    z.stg = carve(1)
    sm = pool[:, 34 * 512:35 * 512]
    z.sqs = z.at
    z.qscol = sm[:, 0:2]
    z.kscol = sm[:, 2:4]
    z.lngcol = sm[:, 4:20]
    z.lnbcol = sm[:, 20:36]
    z.b2col = sm[:, 36:52]
    z.b1acol = sm[:, 52:96]
    z.b1gcol = sm[:, 96:140]
    z.stash = sm[:, 144:172]
    z.stash2 = sm[:, 172:232]
    z.gsc = sm[:, 240:496]
    z.nrow = b.nrow_a
    z.nrowb = b.nrow_b

    with nc.Block() as blk:
        @blk.sync
        def _(sy):
            b.dma(sy, z.qscol, hb["L0_ffn_qsP"].ap())
            b.dma(sy, z.kscol, hb["L0_ffn_ksP"].ap())
            b.dma(sy, z.lngcol, hb["L0_ffn_lngP"].ap())
            b.dma(sy, z.lnbcol, hb["L0_ffn_lnbP"].ap())
            b.dma(sy, z.b2col, hb["L0_ffn_b2P"].ap())
            b.dma(sy, z.b1acol, hb["L0_ffn_b1aP"].ap())
            b.dma(sy, z.b1gcol, hb["L0_ffn_b1gP"].ap())
            b.wdma(sy)

        @blk.gpsimd
        def _(gp):
            b.wt(gp)
            b.wv(gp)
            b.ws(gp)
            for cc in range(NCH):
                b.dma(gp, b.xsave.ap()[cc * 128:(cc + 1) * 128, :],
                      xf[:, cc * T:(cc + 1) * T], b.sem_g)
            b.wdma(gp, b.sem_g)

    _l0f_ln_inplace(b, xf, z)

    for lyr in range(2):
        if L0F_PARTS >= 2 + 2 * lyr:
            _l0f_attn(b, xf, z, lyr)
        if L0F_PARTS >= 3 + 2 * lyr:
            _l0f_mlp(b, xf, z, lyr)

    with nc.Block() as blk:
        for cc in range(NCH):
            @blk.sync
            def _(sy, cc=cc):
                b.wv(sy)
                if cc == 0:
                    b.wt(sy)
                    b.ws(sy)
                b.dma(sy, z.kb[:, 0:T], b.xsave.ap()[cc * 128:(cc + 1) * 128, :])

            @blk.vector
            def _(v, cc=cc):
                b.wdma(v)
                v.tensor_add(xf[:, cc * T:(cc + 1) * T], xf[:, cc * T:(cc + 1) * T],
                             z.kb[:, 0:T])
                b.tick(v, b.sem_v)


def _l0f_ln_inplace(b, xf, z):
    nc = b.nc
    cst = b.cst
    mu_r = z.nrow[0:1, 0:512]
    rstd_r = z.nrowb[0:1, 0:512]
    var_r = z.nrowb[0:1, 0:512]
    with nc.Block() as blk:
        for tt in range(T // 512):
            t0 = tt * 512

            @blk.scalar
            def _(s, t0=t0, tt=tt):
                if tt > 0:
                    b.wt(s)
                else:
                    b.wv(s)
                for cc in range(NCH):
                    s.activation(z.vt[:, cc * 512:(cc + 1) * 512],
                                 xf[:, cc * T + t0:cc * T + t0 + 512], AF.Square)
                b.tick(s, b.sem_s)

            @blk.tensor
            def _(t, t0=t0, tt=tt):
                b.ws(t)
                if tt > 0:
                    b.wv(t)
                for cc in range(NCH):
                    t.matmul(z.ps_s0[0:1, :], cst["c_onescols"][:, 1:2],
                             xf[:, cc * T + t0:cc * T + t0 + 512],
                             start=(cc == 0), stop=(cc == NCH - 1))
                ins = None
                for cc in range(NCH):
                    ins = t.matmul(z.ps_s1[0:1, :], cst["c_onescols"][:, 1:2],
                                   z.vt[:, cc * 512:(cc + 1) * 512],
                                   start=(cc == 0), stop=(cc == NCH - 1))
                b.mmtick(ins)

            @blk.vector
            def _(v):
                b.wt(v)
                v.tensor_copy(mu_r, z.ps_s0[0:1, :])
                v.drain()
                v.tensor_mul(var_r, mu_r, mu_r)
                v.drain()
                v.tensor_sub(var_r, z.ps_s1[0:1, :], var_r)
                b.tick(v, b.sem_v)

            @blk.scalar
            def _(s):
                b.wv(s)
                s.activation(rstd_r, var_r, AF.Sqrt, bias=cst["c_onescols"][0:1, 2:3])
                b.tick(s, b.sem_s)

            @blk.vector
            def _(v):
                b.ws(v)
                v.reciprocal(rstd_r, rstd_r)
                b.tick(v, b.sem_v)

            @blk.tensor
            def _(t):
                b.wv(t)
                t.matmul(z.ps_v0[:], cst["c_ones1x"][0:1, :], mu_r, start=True, stop=True)
                ins = t.matmul(z.ps_v1[:], cst["c_ones1x"][0:1, :], rstd_r, start=True, stop=True)
                b.mmtick(ins)

            @blk.vector
            def _(v, t0=t0, tt=tt):
                b.wt(v)
                if tt == 0:
                    b.wdma(v, b.sem_g)
                for cc in range(NCH):
                    v.tensor_sub(xf[:, cc * T + t0:cc * T + t0 + 512],
                                 xf[:, cc * T + t0:cc * T + t0 + 512], z.ps_v0[:])
                v.drain()
                for cc in range(NCH):
                    v.tensor_mul(xf[:, cc * T + t0:cc * T + t0 + 512],
                                 xf[:, cc * T + t0:cc * T + t0 + 512], z.ps_v1[:])
                b.tick(v, b.sem_v)


def _shift_rhs(b, xf, cc, t0, n):
    if cc < 4:
        return xf[:, cc * T + t0:cc * T + t0 + n], 0
    if t0 == 0:
        return xf[:, cc * T:cc * T + n - 1], 1
    return xf[:, cc * T + t0 - 1:cc * T + t0 - 1 + n], 0


def _l0f_attn(b, xf, z, lyr):
    nc = b.nc
    C = b.C
    cst = b.cst
    hb = b.hb
    W = 512
    NW = T // W
    wqkv = hb["L0_ffn_wqkv"].ap()[lyr]
    wo = hb["L0_ffn_wo"].ap()[lyr]

    with nc.Block() as blk:
        @blk.sync
        def _(sy):
            b.wt(sy)
            b.wv(sy)
            b.ws(sy)

        @blk.scalar
        def _(s):
            b.wt(s)
            b.wv(s)

        @blk.vector
        def _(v):
            b.wt(v)
            b.ws(v)
            for wi in range(1, NW):
                for i in range(4):
                    v.tensor_copy(z.stash[:, (wi - 1) * 4 + i:(wi - 1) * 4 + i + 1],
                                  xf[:, (4 + i) * T + wi * W - 1:(4 + i) * T + wi * W])
            b.tick(v, b.sem_v)

    for w in range(NW):
        t0 = w * W
        par = w % 2
        with nc.Block() as blk:
            for j in range(8):
                ispair_q = j < 4
                pairi = j % 4
                bufp = j % 2

                @blk.sync
                def _(sy, j=j, bufp=bufp, t0=t0):
                    g = b.sg.get(("wsl", bufp), 0)
                    if g:
                        sy.wait_ge(b.sem_t, g)
                    if j == 0:
                        b.dma(sy, z.rt[:, 0:512], hb["c_cs"].ap()[:, t0:t0 + W])
                        b.dma(sy, z.rt[:, 512:1024], hb["c_ss"].ap()[:, t0:t0 + W])
                    if j == 4:
                        sy.wait_ge(b.sem_v, C.get(b.sem_v))
                        b.dma(sy, z.rt[:, 0:512], hb["c_csr"].ap()[:, t0:t0 + W])
                        b.dma(sy, z.rt[:, 512:1024], hb["c_ssr"].ap()[:, t0:t0 + W])
                    b.dma(sy, slab3(z.wsl, bufp * 1024, NCH, 128),
                          wslab_src(wqkv, j * 128, 128))

                @blk.tensor
                def _(t, j=j, bufp=bufp, t0=t0, w=w):
                    b.wdma(t)
                    b.wv(t)
                    lhs_of = lambda cc: z.wsl[:, bufp * 1024 + cc * 128:
                                              bufp * 1024 + (cc + 1) * 128]
                    mms = []
                    for cc in range(NCH):
                        if cc < 4:
                            mms.append((lhs_of(cc), xf[:, cc * T + t0:cc * T + t0 + W],
                                        z.ps_s0[:, 0:W]))
                        else:
                            mms.append((lhs_of(cc),
                                        xf[:, cc * T + t0:cc * T + t0 + W - 1],
                                        z.ps_s0[:, 1:W]))
                            if w > 0:
                                si = (w - 1) * 4 + (cc - 4)
                                mms.append((lhs_of(cc), z.stash[:, si:si + 1],
                                            z.ps_s0[:, 0:1]))
                    ins = None
                    for i, (lh, rh, out) in enumerate(mms):
                        ins = t.matmul(out, lh, rh, start=(i == 0),
                                       stop=(i == len(mms) - 1))
                    b.mmtick(ins)
                    b.sg[("wsl", bufp)] = b.C.get(b.sem_t)

                @blk.scalar
                def _(s):
                    b.wt(s)
                    s.activation(z.sqs[:, 0:W], z.ps_s0[:, 0:W], AF.Square)
                    b.tick(s, b.sem_s)

                @blk.tensor
                def _(t):
                    b.ws(t)
                    ins = t.matmul(z.ps_d[0:2, 0:W], cst["c_ones2"], z.sqs[:, 0:W],
                                   start=True, stop=True)
                    b.mmtick(ins)

                @blk.scalar
                def _(s):
                    b.wt(s)
                    s.activation(z.nrow[0:2, :], z.ps_d[0:2, 0:W], AF.Sqrt)
                    b.tick(s, b.sem_s)

                @blk.vector
                def _(v):
                    b.ws(v)
                    v.tensor_scalar(z.nrow[0:2, :], z.nrow[0:2, :], 1e-12, None, ALU.max)
                    v.drain()
                    v.reciprocal(z.nrow[0:2, :], z.nrow[0:2, :])
                    b.tick(v, b.sem_v)

                @blk.tensor
                def _(t):
                    b.wv(t)
                    ins = t.matmul(z.ps_d[:, 0:W], cst["c_sel2"], z.nrow[0:2, :],
                                   start=True, stop=True)
                    b.mmtick(ins)

                @blk.vector
                def _(v, ispair_q=ispair_q, lyr=lyr):
                    b.wt(v)
                    scol = z.qscol[:, lyr:lyr + 1] if ispair_q else z.kscol[:, lyr:lyr + 1]
                    v.tensor_copy(z.sqs[:, 0:W], z.ps_d[:, 0:W])
                    v.drain()
                    v.tensor_mul(z.sqs[:, 0:W], z.ps_s0[:, 0:W], z.sqs[:, 0:W])
                    v.drain()
                    if ispair_q:
                        v.tensor_scalar(z.sqs[:, 0:W], z.sqs[:, 0:W], scol, 8.0,
                                        ALU.mult, ALU.mult)
                    else:
                        v.tensor_scalar(z.sqs[:, 0:W], z.sqs[:, 0:W], scol, None, ALU.mult)
                    b.tick(v, b.sem_v)

                @blk.tensor
                def _(t):
                    b.wv(t)
                    ins = t.matmul(z.ps_s1[:, 0:W], cst["c_rot"], z.sqs[:, 0:W],
                                   start=True, stop=True)
                    b.mmtick(ins)

                @blk.vector
                def _(v, ispair_q=ispair_q, pairi=pairi, par=par):
                    b.wt(v)
                    cosn = z.rt[:, 0:512]
                    sinn = z.rt[:, 512:1024]
                    if ispair_q:
                        dst = z.qb[:, pairi * 512:(pairi + 1) * 512]
                    else:
                        dst = z.kb[:, pairi * 1024 + par * 512:pairi * 1024 + par * 512 + 512]
                    v.tensor_mul(dst, z.sqs[:, 0:W], cosn)
                    v.drain()
                    v.tensor_mul(z.sqs[:, 0:W], z.ps_s1[:, 0:W], sinn)
                    v.drain()
                    v.tensor_add(dst, dst, z.sqs[:, 0:W])
                    b.tick(v, b.sem_v)

            @blk.vector
            def _(v, w=w, t0=t0):
                g = b.sg.get("stg", 0)
                if g:
                    v.wait_ge(b.sem_t, g)
                for i, cc in enumerate(range(4, 8)):
                    dst = z.stg[:, i * 128:(i + 1) * 128]
                    if w == 0:
                        v.memset(dst[:, 0:1], 0.0)
                    else:
                        si = (w - 1) * 4 + i
                        v.tensor_copy(dst[:, 0:1], z.stash[:, si:si + 1])
                    v.tensor_copy(dst[:, 1:128], xf[:, cc * T + t0:cc * T + t0 + 127])
                b.tick(v, b.sem_v)

            for dc in range(NCH):
                bufp = dc % 2

                @blk.sync
                def _(sy, dc=dc, bufp=bufp):
                    g = b.sg.get(("wsl", bufp), 0)
                    if g:
                        sy.wait_ge(b.sem_t, g)
                    b.dma(sy, z.wsl[:, bufp * 1024:bufp * 1024 + 512],
                          wqkv[dc * 128:(dc + 1) * 128, 1024:1536])

                @blk.tensor
                def _(t, dc=dc, bufp=bufp, t0=t0):
                    b.wdma(t)
                    if dc == 0:
                        b.wv(t)
                    ins = None
                    for tc in range(4):
                        pst = (z.ps_s0, z.ps_s1, z.ps_o, z.ps_d)[tc]
                        tt0 = t0 + tc * 128
                        if dc < 4:
                            lhs = xf[:, dc * T + tt0:dc * T + tt0 + 128]
                        elif tc == 0:
                            lhs = z.stg[:, (dc - 4) * 128:(dc - 3) * 128]
                        else:
                            lhs = xf[:, dc * T + tt0 - 1:dc * T + tt0 + 127]
                        ins = t.matmul(pst[:, 0:512], lhs,
                                       z.wsl[:, bufp * 1024:bufp * 1024 + 512],
                                       start=(dc == 0), stop=(dc == NCH - 1))
                    b.mmtick(ins)
                    b.sg[("wsl", bufp)] = b.C.get(b.sem_t)
                    if dc >= 4:
                        b.sg["stg"] = b.C.get(b.sem_t)

            @blk.vector
            def _(v, par=par):
                b.wt(v)
                for tc in range(4):
                    pst = (z.ps_s0, z.ps_s1, z.ps_o, z.ps_d)[tc]
                    blkbase = (par * 4 + tc) * 520
                    dst = AP(z.vt.tensor, z.vt.offset + blkbase,
                             [[z.vt.tensor.shape[1], 128], [65, 8], [1, 64]])
                    v.tensor_copy(dst, pst[:, 0:512].rearrange("p (a b) -> p a b", b=64))
                    ones = AP(z.vt.tensor, z.vt.offset + blkbase + 64,
                              [[z.vt.tensor.shape[1], 128], [65, 8]])
                    v.memset(ones, 1.0)
                b.tick(v, b.sem_v)

            for pairi in range(4):
                for hh in range(2):
                    head = 2 * pairi + hh
                    kcs = [(par, 0, 0, W, "cur")]
                    if w > 0:
                        for tc in range(4):
                            kcs.append((1 - par, tc, 0, tc * 128 + 128, "prev"))
                    for tc in range(1, 4):
                        kcs.append((par, tc, tc * 128, W - tc * 128, "cur"))
                    nkc = len(kcs)
                    for kci, (kpar, tc, q0, qn, kind) in enumerate(kcs):
                        pss = z.ps_s0 if kci % 2 == 0 else z.ps_s1

                        @blk.tensor
                        def _(t, pairi=pairi, hh=hh, kpar=kpar, tc=tc, q0=q0, qn=qn,
                              pss=pss, kci=kci):
                            if kci == 0:
                                b.wv(t)
                            if kci >= 2:
                                b.ws(t)
                            lhs = z.kb[64 * hh:64 * hh + 64,
                                       pairi * 1024 + kpar * 512 + tc * 128:
                                       pairi * 1024 + kpar * 512 + (tc + 1) * 128]
                            rhs = z.qb[64 * hh:64 * hh + 64,
                                       pairi * 512 + q0:pairi * 512 + q0 + qn]
                            ins = t.matmul(pss[:, 0:qn], lhs, rhs, start=True, stop=True)
                            b.mmtick(ins)

                        @blk.vector
                        def _(v, pss=pss, qn=qn, kind=kind, tc=tc):
                            b.wt(v)
                            if kind == "cur":
                                v.tensor_add(pss[:, 0:128], pss[:, 0:128], cst["c_mcur"])
                            else:
                                v.tensor_add(pss[:, tc * 128:tc * 128 + 128],
                                             pss[:, tc * 128:tc * 128 + 128], cst["c_mprev"])
                            b.tick(v, b.sem_v)

                        @blk.scalar
                        def _(s, pss=pss, qn=qn):
                            b.wv(s)
                            s.activation(z.at[:, 0:qn], pss[:, 0:qn], AF.Exp,
                                         bias=cst["c_onescols"][:, 4:5])
                            b.tick(s, b.sem_s)

                        @blk.tensor
                        def _(t, kpar=kpar, tc=tc, head=head, hh=hh, q0=q0, qn=qn,
                              kci=kci, nkc=nkc):
                            b.ws(t)
                            last = kci == nkc - 1
                            vsl = z.vt[:, (kpar * 4 + tc) * 520 + head * 65:
                                       (kpar * 4 + tc) * 520 + head * 65 + 64]
                            psd = z.ps_v0 if hh == 0 else z.ps_v1
                            t.matmul(z.ps_o[64 * hh:64 * hh + 64, q0:q0 + qn], vsl,
                                     z.at[:, 0:qn], start=(kci == 0), stop=last,
                                     tile_position=(0, 64 * hh))
                            ins = t.matmul(psd[0:1, q0:q0 + qn],
                                           cst["c_onescols"][:, 0:1], z.at[:, 0:qn],
                                           start=(kci == 0), stop=last)
                            b.mmtick(ins)

                @blk.vector
                def _(v):
                    b.wt(v)
                    v.reciprocal(z.nrow[0:1, 0:512], z.ps_v0[0:1, 0:512])
                    v.reciprocal(z.nrowb[0:1, 0:512], z.ps_v1[0:1, 0:512])
                    b.tick(v, b.sem_v)

                @blk.tensor
                def _(t):
                    b.wv(t)
                    t.matmul(z.ps_d[:, 0:512], cst["c_sello"], z.nrow[0:1, 0:512],
                             start=True, stop=False)
                    ins = t.matmul(z.ps_d[:, 0:512], cst["c_selhi"], z.nrowb[0:1, 0:512],
                                   start=False, stop=True)
                    b.mmtick(ins)

                @blk.vector
                def _(v, pairi=pairi):
                    b.wt(v)
                    v.tensor_copy(z.at[:, 0:512], z.ps_d[:, 0:512])
                    v.drain()
                    v.tensor_mul(z.ob[:, pairi * 512:(pairi + 1) * 512],
                                 z.ps_o[:, 0:512], z.at[:, 0:512])
                    b.tick(v, b.sem_v)

            for j in range(NCH):
                bufp = j % 2

                @blk.sync
                def _(sy, j=j, bufp=bufp):
                    g = b.sg.get(("wsl", bufp), 0)
                    if g:
                        sy.wait_ge(b.sem_t, g)
                    b.dma(sy, slab3(z.wsl, bufp * 1024, 4, 128), wslab_src(wo, j * 128, 128))

                @blk.tensor
                def _(t, j=j, bufp=bufp):
                    b.wdma(t)
                    b.wv(t)
                    ins = None
                    for p4 in range(4):
                        ins = t.matmul(z.ps_s0[:, 0:W],
                                       z.wsl[:, bufp * 1024 + p4 * 128:bufp * 1024 + (p4 + 1) * 128],
                                       z.ob[:, p4 * 512:(p4 + 1) * 512],
                                       start=(p4 == 0), stop=(p4 == 3))
                    b.mmtick(ins)
                    b.sg[("wsl", bufp)] = b.C.get(b.sem_t)

                @blk.vector
                def _(v, j=j, t0=t0):
                    b.wt(v)
                    v.tensor_add(xf[:, j * T + t0:j * T + t0 + W],
                                 xf[:, j * T + t0:j * T + t0 + W], z.ps_s0[:, 0:W])
                    b.tick(v, b.sem_v)

        if DBG_DUMP == ("l0f_w", lyr, w):
            with nc.Block() as blk2:
                @blk2.gpsimd
                def _(gp):
                    b.wt(gp)
                    b.wv(gp)
                    b.ws(gp)
                    oa = b.out_h.ap()
                    b.dma(gp, oa[0:128, 0:2048], z.qb[:, :], b.sem_g)
                    b.dma(gp, oa[128:256, 0:4096], z.kb[:, :], b.sem_g)
                    b.dma(gp, oa[256:384, 0:4096], z.vt[:, 0:4096], b.sem_g)
                    b.dma(gp, oa[384:512, 0:2048], z.ob[:, :], b.sem_g)
                    b.dma(gp, oa[512:640, 0:4096], xf[:, 0:T], b.sem_g)
                    b.dma(gp, oa[640:768, 0:2048], z.wsl[:, :], b.sem_g)
                    b.wdma(gp, b.sem_g)
            raise DbgStop()


def _l0f_mlp(b, xf, z, lyr):
    nc = b.nc
    C = b.C
    cst = b.cst
    hb = b.hb
    ST = 256
    w1 = hb["L0_ffn_w1"].ap()[lyr]
    w2 = hb["L0_ffn_w2"].ap()[lyr]
    NJ = 22
    cw = lambda j: 42 if j == 21 else 128

    pool = b.pool
    h1full = pool[:, 0:5632]
    xn2 = pool[:, 5632:7680]
    sqbig = xn2
    w1buf = lambda bp: pool[:, 7680 + bp * 2048:7680 + (bp + 1) * 2048]
    w2buf = lambda bp: pool[:, 11776 + bp * 2816:11776 + (bp + 1) * 2816]

    def h1sl(j, n):
        return h1full[:, j * 256:j * 256 + n]

    mu_r = z.nrow[0:1, 0:ST]
    rstd_r = z.nrowb[0:1, 0:ST]
    var_r = z.nrowb[0:1, 0:ST]

    with nc.Block() as blk:
        @blk.sync
        def _(sy):
            b.wt(sy)
            b.wv(sy)
            b.ws(sy)

        @blk.scalar
        def _(s):
            b.wt(s)
            b.wv(s)

        @blk.vector
        def _(v):
            b.wt(v)
            b.ws(v)
            for st in range(1, T // ST):
                for i in range(4):
                    v.tensor_copy(z.stash2[:, (st - 1) * 4 + i:(st - 1) * 4 + i + 1],
                                  xf[:, (4 + i) * T + st * ST - 1:(4 + i) * T + st * ST])
            b.tick(v, b.sem_v)
            b.sg["stash2v"] = b.C.get(b.sem_v)

    for st in range(T // ST):
        t0 = st * ST
        with nc.Block() as blk:
            @blk.scalar
            def _(s, t0=t0, st=st):
                if t0 > 0:
                    s.wait_ge(b.sem_v, b.sg["stash2v"])
                g = b.sg.get("mlp_h1v", 0)
                if g:
                    s.wait_ge(b.sem_v, g)
                for cc in range(NCH):
                    if cc < 4:
                        s.activation(sqbig[:, cc * ST:(cc + 1) * ST],
                                     xf[:, cc * T + t0:cc * T + t0 + ST], AF.Square)
                    else:
                        if t0 > 0:
                            si = (st - 1) * 4 + (cc - 4)
                            s.activation(sqbig[:, cc * ST:cc * ST + 1],
                                         z.stash2[:, si:si + 1], AF.Square)
                        s.activation(sqbig[:, cc * ST + 1:(cc + 1) * ST],
                                     xf[:, cc * T + t0:cc * T + t0 + ST - 1], AF.Square)
                b.tick(s, b.sem_s)

            @blk.vector
            def _(v, t0=t0):
                if t0 == 0:
                    for cc in range(4, NCH):
                        v.memset(sqbig[:, cc * ST:cc * ST + 1], 0.0)
                b.tick(v, b.sem_v)

            @blk.tensor
            def _(t, t0=t0, st=st):
                b.ws(t)
                b.wv(t)
                mms = []
                for cc in range(NCH):
                    if cc < 4:
                        mms.append((xf[:, cc * T + t0:cc * T + t0 + ST],
                                    z.ps_s0[0:1, 0:ST]))
                    else:
                        mms.append((xf[:, cc * T + t0:cc * T + t0 + ST - 1],
                                    z.ps_s0[0:1, 1:ST]))
                        if t0 > 0:
                            si = (st - 1) * 4 + (cc - 4)
                            mms.append((z.stash2[:, si:si + 1], z.ps_s0[0:1, 0:1]))
                for i, (rh, out) in enumerate(mms):
                    t.matmul(out, cst["c_onescols"][:, 1:2], rh,
                             start=(i == 0), stop=(i == len(mms) - 1))
                ins = None
                for cc in range(NCH):
                    ins = t.matmul(z.ps_s1[0:1, 0:ST], cst["c_onescols"][:, 1:2],
                                   sqbig[:, cc * ST:(cc + 1) * ST],
                                   start=(cc == 0), stop=(cc == NCH - 1))
                b.mmtick(ins)

            @blk.vector
            def _(v):
                b.wt(v)
                v.tensor_copy(mu_r, z.ps_s0[0:1, 0:ST])
                v.drain()
                v.tensor_mul(var_r, mu_r, mu_r)
                v.drain()
                v.tensor_sub(var_r, z.ps_s1[0:1, 0:ST], var_r)
                b.tick(v, b.sem_v)

            @blk.scalar
            def _(s):
                b.wv(s)
                s.activation(rstd_r, var_r, AF.Sqrt, bias=cst["c_onescols"][0:1, 2:3])
                b.tick(s, b.sem_s)

            @blk.vector
            def _(v):
                b.ws(v)
                v.reciprocal(rstd_r, rstd_r)
                b.tick(v, b.sem_v)

            @blk.tensor
            def _(t):
                b.wv(t)
                t.matmul(z.ps_v0[:, 0:ST], cst["c_ones1x"][0:1, :], mu_r, start=True, stop=True)
                ins = t.matmul(z.ps_v1[:, 0:ST], cst["c_ones1x"][0:1, :], rstd_r,
                               start=True, stop=True)
                b.mmtick(ins)

            @blk.vector
            def _(v, t0=t0, lyr=lyr, st=st):
                b.wt(v)
                for cc in range(NCH):
                    dst = xn2[:, cc * ST:(cc + 1) * ST]
                    if cc < 4:
                        v.tensor_sub(dst, xf[:, cc * T + t0:cc * T + t0 + ST],
                                     z.ps_v0[:, 0:ST])
                    elif t0 == 0:
                        v.tensor_sub(dst[:, 1:ST], xf[:, cc * T:cc * T + ST - 1],
                                     z.ps_v0[:, 1:ST])
                        v.drain()
                        v.tensor_scalar(dst[:, 0:1], z.ps_v0[:, 0:1], -1.0, None, ALU.mult)
                    else:
                        si = (st - 1) * 4 + (cc - 4)
                        v.tensor_sub(dst[:, 1:ST], xf[:, cc * T + t0:cc * T + t0 + ST - 1],
                                     z.ps_v0[:, 1:ST])
                        v.drain()
                        v.tensor_sub(dst[:, 0:1], z.stash2[:, si:si + 1],
                                     z.ps_v0[:, 0:1])
                v.drain()
                for cc in range(NCH):
                    v.tensor_mul(xn2[:, cc * ST:(cc + 1) * ST],
                                 xn2[:, cc * ST:(cc + 1) * ST], z.ps_v1[:, 0:ST])
                v.drain()
                for cc in range(NCH):
                    v.tensor_scalar(xn2[:, cc * ST:(cc + 1) * ST],
                                    xn2[:, cc * ST:(cc + 1) * ST],
                                    z.lngcol[:, lyr * 8 + cc:lyr * 8 + cc + 1],
                                    z.lnbcol[:, lyr * 8 + cc:lyr * 8 + cc + 1],
                                    ALU.mult, ALU.add)
                b.tick(v, b.sem_v)

            for j in range(NJ):
                n = cw(j)
                bufp = j % 2

                @blk.sync
                def _(sy, j=j, n=n, bufp=bufp):
                    g = b.sg.get(("w1", bufp), 0)
                    if g:
                        sy.wait_ge(b.sem_t, g)
                    wb = w1buf(bufp)
                    b.dma(sy, slab3(wb, 0, NCH, 128)[:, :, 0:n],
                          wslab_src(w1, j * 128, n))
                    b.dma(sy, slab3(wb, 1024, NCH, 128)[:, :, 0:n],
                          wslab_src(w1, 2730 + j * 128, n))

                @blk.tensor
                def _(t, j=j, n=n, bufp=bufp):
                    b.wdma(t)
                    b.ws(t)
                    b.wv(t)
                    wb = w1buf(bufp)
                    for cc in range(NCH):
                        t.matmul(z.ps_s0[0:n, 0:ST],
                                 wb[:, cc * 128:cc * 128 + n],
                                 xn2[:, cc * ST:(cc + 1) * ST],
                                 start=(cc == 0), stop=(cc == NCH - 1))
                    ins = None
                    for cc in range(NCH):
                        ins = t.matmul(
                            z.ps_s1[0:n, 0:ST],
                            wb[:, 1024 + cc * 128:1024 + cc * 128 + n],
                            xn2[:, cc * ST:(cc + 1) * ST],
                            start=(cc == 0), stop=(cc == NCH - 1))
                    b.mmtick(ins)
                    b.sg[("w1", bufp)] = b.C.get(b.sem_t)

                @blk.scalar
                def _(s, j=j, n=n, lyr=lyr):
                    b.wt(s)
                    b.wv(s)
                    s.activation(z.gsc[0:n, 0:ST], z.ps_s1[0:n, 0:ST], AF.Gelu,
                                 bias=z.b1gcol[0:n, lyr * 22 + j:lyr * 22 + j + 1])
                    b.tick(s, b.sem_s)

                @blk.vector
                def _(v, j=j, n=n, lyr=lyr):
                    b.ws(v)
                    v.scalar_tensor_tensor(h1sl(j, ST)[0:n, :], z.ps_s0[0:n, 0:ST],
                                           z.b1acol[0:n, lyr * 22 + j:lyr * 22 + j + 1],
                                           z.gsc[0:n, 0:ST], ALU.add, ALU.mult)
                    b.tick(v, b.sem_v)
                    if j == NJ - 1:
                        b.sg["mlp_h1v"] = b.C.get(b.sem_v)

            for oj in range(NCH):
                bufp = oj % 2

                @blk.sync
                def _(sy, oj=oj, bufp=bufp):
                    g = b.sg.get(("w2", bufp), 0)
                    if g:
                        sy.wait_ge(b.sem_t, g)
                    wb = w2buf(bufp)
                    b.dma(sy, slab3(wb, 0, 21, 128), wslab_src2(w2[0:2688, :], oj * 128, 128))
                    b.dma(sy, wb[0:42, 21 * 128:22 * 128],
                          w2[2688:2730, oj * 128:(oj + 1) * 128])

                @blk.tensor
                def _(t, oj=oj, bufp=bufp):
                    b.wdma(t)
                    b.wv(t)
                    wb = w2buf(bufp)
                    ins = None
                    for j in range(NJ):
                        n = cw(j)
                        ins = t.matmul(z.ps_o[:, 0:ST],
                                       wb[0:n, j * 128:j * 128 + 128],
                                       h1sl(j, ST)[0:n, :],
                                       start=(j == 0), stop=(j == NJ - 1))
                    b.mmtick(ins)
                    b.sg[("w2", bufp)] = b.C.get(b.sem_t)

                @blk.vector
                def _(v, oj=oj, t0=t0, lyr=lyr):
                    b.wt(v)
                    v.scalar_tensor_tensor(xf[:, oj * T + t0:oj * T + t0 + ST],
                                           z.ps_o[:, 0:ST],
                                           z.b2col[:, lyr * 8 + oj:lyr * 8 + oj + 1],
                                           xf[:, oj * T + t0:oj * T + t0 + ST],
                                           ALU.add, ALU.add)
                    b.tick(v, b.sem_v)


def wslab_src2(ap2d, j0, ncols):
    return ap2d[:, j0:j0 + ncols].rearrange("(c p) n -> p c n", p=128)


def stage_l1_attn(b, xf):
    nc = b.nc
    hb = b.hb
    cst = b.cst
    C = b.C
    z = types.SimpleNamespace()
    pool = b.pool
    cur = [0]

    def carve(units):
        o = cur[0]
        cur[0] += units * 512
        return pool[:, o:o + units * 512]

    z.smean = carve(1)
    z.kfm = carve(1)
    z.vtm = carve(2)
    z.qfm = carve(8)
    z.at = carve(1)
    z.at2 = carve(1)
    z.msk = carve(1)
    z.ob = carve(8)
    z.wsl = carve(4)
    sm3 = pool[:, 13824:14336]
    z.bqc = sm3[:, 0:8]
    z.bkc = sm3[:, 8:16]
    z.boc = sm3[:, 16:24]
    z.bvrow = pool[0:1, 14336:15360]
    z.ps_s0 = b.pss[0][:, :]
    z.ps_s1 = b.pss[1][:, :]
    z.ps_o = b.pss[2][:, :]
    z.ps_d = b.pss[3][:, :]
    z.ps_v0 = b.pss[4][:, :]
    z.ps_v1 = b.pss[5][:, :]
    z.vt = z.ob
    z.nrow = b.nrow_a
    z.nrowb = b.nrow_b

    with nc.Block() as blk:
        @blk.sync
        def _(sy):
            b.wt(sy)
            b.wv(sy)
            b.ws(sy)
            b.dma(sy, z.bqc, hb["L1_attn_bqC"].ap())
            b.dma(sy, z.bkc, hb["L1_attn_bkC"].ap())
            b.dma(sy, z.boc, hb["L1_attn_boC"].ap())
            b.dma(sy, z.bvrow, hb["L1_attn_bvR"].ap())
            b.wdma(sy)

        @blk.scalar
        def _(s):
            b.wt(s)
            b.wv(s)

        @blk.vector
        def _(v):
            b.wt(v)
            b.ws(v)

        @blk.gpsimd
        def _(gp):
            b.wt(gp)
            b.wv(gp)
            b.ws(gp)
            for cc in range(NCH):
                b.dma(gp, b.xsave.ap()[cc * 128:(cc + 1) * 128, :],
                      xf[:, cc * T:(cc + 1) * T], b.sem_g)
            b.wdma(gp, b.sem_g)

    _l0f_ln_inplace(b, xf, z)

    with nc.Block() as blk:
        @blk.vector
        def _(v):
            b.wt(v)
            for cc in range(NCH):
                v.tensor_copy(z.smean[:, cc * 64:cc * 64 + 1], xf[:, cc * T:cc * T + 1])
                src3 = xf[:, cc * T + 1:cc * T + 4033].rearrange("p (a b) -> p a b", b=64)
                v.reduce_sum(z.smean[:, cc * 64 + 1:cc * 64 + 64], src3, axis=AX.X)
            v.drain()
            sc3 = AP(z.smean.tensor, z.smean.offset + 1,
                     [[z.smean.tensor.shape[1], 128], [64, NCH], [1, 63]])
            v.tensor_scalar(sc3, sc3, 1.0 / 64.0, None, ALU.mult)
            b.tick(v, b.sem_v)

    with nc.Block() as blk:
        for j in range(NCH):
            bufp = j % 2

            @blk.sync
            def _(sy, j=j, bufp=bufp):
                g = b.sg.get(("wsl", bufp), 0)
                if g:
                    sy.wait_ge(b.sem_t, g)
                b.dma(sy, slab3(z.wsl, bufp * 1024, NCH, 128),
                      wslab_src(hb["L1_attn_Wk"].ap(), j * 128, 128))

            @blk.tensor
            def _(t, j=j, bufp=bufp):
                b.wdma(t)
                b.wv(t)
                pss = z.ps_s0 if j % 2 == 0 else z.ps_s1
                ins = None
                for cc in range(NCH):
                    ins = t.matmul(pss[:, 0:64],
                                   z.wsl[:, bufp * 1024 + cc * 128:bufp * 1024 + (cc + 1) * 128],
                                   z.smean[:, cc * 64:(cc + 1) * 64],
                                   start=(cc == 0), stop=(cc == NCH - 1))
                b.mmtick(ins)
                b.sg[("wsl", bufp)] = b.C.get(b.sem_t)

            @blk.vector
            def _(v, j=j):
                b.wt(v)
                pss = z.ps_s0 if j % 2 == 0 else z.ps_s1
                v.tensor_scalar(z.kfm[:, j * 64:(j + 1) * 64], pss[:, 0:64],
                                z.bkc[:, j:j + 1], None, ALU.add)
                b.tick(v, b.sem_v)

        for cc in range(NCH):
            bufp = cc % 2

            @blk.sync
            def _(sy, cc=cc, bufp=bufp):
                g = b.sg.get(("wsl", bufp), 0)
                if g:
                    sy.wait_ge(b.sem_t, g)
                b.dma(sy, z.wsl[:, bufp * 1024:(bufp + 1) * 1024],
                      hb["L1_attn_Wv"].ap()[cc * 128:(cc + 1) * 128, :])

            @blk.tensor
            def _(t, cc=cc, bufp=bufp):
                b.wdma(t)
                if cc == 0:
                    b.wv(t)
                ins = None
                for hf in range(2):
                    pss = z.ps_v0 if hf == 0 else z.ps_v1
                    ins = t.matmul(pss[0:64, 0:512], z.smean[:, cc * 64:(cc + 1) * 64],
                                   z.wsl[:, bufp * 1024 + hf * 512:bufp * 1024 + hf * 512 + 512],
                                   start=(cc == 0), stop=False)
                    if cc == NCH - 1:
                        ins = t.matmul(pss[0:64, 0:512], cst["c_ones1x"][0:1, 0:64],
                                       z.bvrow[0:1, hf * 512:hf * 512 + 512],
                                       start=False, stop=True)
                b.mmtick(ins)
                b.sg[("wsl", bufp)] = b.C.get(b.sem_t)

        @blk.vector
        def _(v):
            b.wt(v)
            v.tensor_copy(z.vtm[0:64, 0:512], z.ps_v0[0:64, 0:512])
            v.tensor_copy(z.vtm[0:64, 512:1024], z.ps_v1[0:64, 0:512])
            b.tick(v, b.sem_v)

    W = 512
    NW = T // W
    for w in range(NW):
        t0 = w * W
        with nc.Block() as blk:
            @blk.sync
            def _(sy, w=w):
                g = b.sg.get("l1msk", 0)
                if g:
                    sy.wait_ge(b.sem_v, g)
                b.dma(sy, z.msk[0:64, 0:512], hb["c_l1mask"].ap()[:, w * 512:(w + 1) * 512])
                b.wdma(sy)

            for j in range(NCH):
                bufp = j % 2

                @blk.sync
                def _(sy, j=j, bufp=bufp):
                    g = b.sg.get(("wsl", bufp), 0)
                    if g:
                        sy.wait_ge(b.sem_t, g)
                    b.dma(sy, slab3(z.wsl, bufp * 1024, NCH, 128),
                          wslab_src(hb["L1_attn_Wq"].ap(), j * 128, 128))

                @blk.tensor
                def _(t, j=j, bufp=bufp, t0=t0):
                    b.wdma(t)
                    b.wv(t)
                    pss = z.ps_s0 if j % 2 == 0 else z.ps_s1
                    ins = None
                    for cc in range(NCH):
                        ins = t.matmul(pss[:, 0:W],
                                       z.wsl[:, bufp * 1024 + cc * 128:bufp * 1024 + (cc + 1) * 128],
                                       xf[:, cc * T + t0:cc * T + t0 + W],
                                       start=(cc == 0), stop=(cc == NCH - 1))
                    b.mmtick(ins)
                    b.sg[("wsl", bufp)] = b.C.get(b.sem_t)

                @blk.vector
                def _(v, j=j):
                    b.wt(v)
                    pss = z.ps_s0 if j % 2 == 0 else z.ps_s1
                    v.tensor_scalar(z.qfm[:, j * 512:(j + 1) * 512], pss[:, 0:W],
                                    z.bqc[:, j:j + 1], 0.125, ALU.add, ALU.mult)
                    b.tick(v, b.sem_v)

            for p in range(NPAIR):
                @blk.tensor
                def _(t, p=p):
                    b.wv(t)
                    for hh in range(2):
                        pss = z.ps_s0 if hh == 0 else z.ps_s1
                        ins = t.matmul(pss[0:64, 0:512],
                                       z.kfm[64 * hh:64 * hh + 64, p * 64:(p + 1) * 64],
                                       z.qfm[64 * hh:64 * hh + 64, p * 512:(p + 1) * 512],
                                       start=True, stop=True,
                                       tile_position=(64 * hh, 0))
                    b.mmtick(ins)

                @blk.vector
                def _(v, p=p, w=w):
                    b.wt(v)
                    v.tensor_add(z.ps_s0[0:64, 0:512], z.ps_s0[0:64, 0:512], z.msk[0:64, 0:512])
                    v.tensor_add(z.ps_s1[0:64, 0:512], z.ps_s1[0:64, 0:512], z.msk[0:64, 0:512])
                    b.tick(v, b.sem_v)
                    if p == NPAIR - 1:
                        b.sg["l1msk"] = b.C.get(b.sem_v)

                @blk.scalar
                def _(s, p=p):
                    b.wv(s)
                    s.activation(z.at[0:64, 0:512], z.ps_s0[0:64, 0:512], AF.Exp,
                                 bias=cst["c_onescols"][0:64, 4:5])
                    s.activation(z.at2[0:64, 0:512], z.ps_s1[0:64, 0:512], AF.Exp,
                                 bias=cst["c_onescols"][0:64, 4:5])
                    b.tick(s, b.sem_s)

                @blk.tensor
                def _(t, p=p):
                    b.ws(t)
                    for hh in range(2):
                        at = z.at if hh == 0 else z.at2
                        hd = 2 * p + hh
                        t.matmul(z.ps_o[64 * hh:64 * hh + 64, 0:512],
                                 z.vtm[0:64, hd * 64:(hd + 1) * 64], at[0:64, 0:512],
                                 start=True, stop=True, tile_position=(0, 64 * hh))
                        psd = z.ps_v0 if hh == 0 else z.ps_v1
                        ins = t.matmul(psd[0:1, 0:512], cst["c_onescols"][0:64, 0:1],
                                       at[0:64, 0:512], start=True, stop=True)
                    b.mmtick(ins)

                @blk.vector
                def _(v):
                    b.wt(v)
                    v.reciprocal(z.nrow[0:1, 0:512], z.ps_v0[0:1, 0:512])
                    v.reciprocal(z.nrowb[0:1, 0:512], z.ps_v1[0:1, 0:512])
                    b.tick(v, b.sem_v)

                @blk.tensor
                def _(t):
                    b.wv(t)
                    t.matmul(z.ps_d[:, 0:512], cst["c_sello"], z.nrow[0:1, 0:512],
                             start=True, stop=False)
                    ins = t.matmul(z.ps_d[:, 0:512], cst["c_selhi"], z.nrowb[0:1, 0:512],
                                   start=False, stop=True)
                    b.mmtick(ins)

                @blk.vector
                def _(v, p=p):
                    b.wt(v)
                    v.tensor_copy(z.at[:, 0:512], z.ps_d[:, 0:512])
                    v.drain()
                    v.tensor_mul(z.ob[:, p * 512:(p + 1) * 512], z.ps_o[:, 0:512],
                                 z.at[:, 0:512])
                    b.tick(v, b.sem_v)

            for oj in range(NCH):
                bufp = oj % 2

                @blk.sync
                def _(sy, oj=oj, bufp=bufp):
                    g = b.sg.get(("wsl", bufp), 0)
                    if g:
                        sy.wait_ge(b.sem_t, g)
                    b.dma(sy, slab3(z.wsl, bufp * 1024, NCH, 128),
                          wslab_src(hb["L1_attn_Wo"].ap(), oj * 128, 128))

                @blk.tensor
                def _(t, oj=oj, bufp=bufp):
                    b.wdma(t)
                    b.wv(t)
                    ins = None
                    for p in range(NPAIR):
                        ins = t.matmul(z.ps_s0[:, 0:W],
                                       z.wsl[:, bufp * 1024 + p * 128:bufp * 1024 + (p + 1) * 128],
                                       z.ob[:, p * 512:(p + 1) * 512],
                                       start=(p == 0), stop=(p == NPAIR - 1))
                    b.mmtick(ins)
                    b.sg[("wsl", bufp)] = b.C.get(b.sem_t)

                @blk.vector
                def _(v, oj=oj, t0=t0):
                    b.wt(v)
                    v.tensor_scalar(xf[:, oj * T + t0:oj * T + t0 + W], z.ps_s0[:, 0:W],
                                    z.boc[:, oj:oj + 1], None, ALU.add)
                    b.tick(v, b.sem_v)

    with nc.Block() as blk:
        for cc in range(NCH):
            @blk.sync
            def _(sy, cc=cc):
                b.wv(sy)
                if cc == 0:
                    b.wt(sy)
                    b.ws(sy)
                b.dma(sy, z.ob[:, 0:T], b.xsave.ap()[cc * 128:(cc + 1) * 128, :])

            @blk.vector
            def _(v, cc=cc):
                b.wdma(v)
                v.tensor_add(xf[:, cc * T:(cc + 1) * T], xf[:, cc * T:(cc + 1) * T],
                             z.ob[:, 0:T])
                b.tick(v, b.sem_v)


MAGIC1 = 12582912.0
MAGIC2 = 6755399441055744.0
MAGIC3 = 12884901888.0


def _pmax_bc(b, blk, src, dst, cols4, cst):
    nc = b.nc
    psA = b.pss[2][:, :]
    psB = b.pss[3][:, :]

    @blk.tensor
    def _(t):
        b.wv(t)
        ins = None
        for tt in range(4):
            ins = t.transpose(psA[:, tt * 128:(tt + 1) * 128],
                              src[:, tt * 128:(tt + 1) * 128], cst["c_id"])
        b.mmtick(ins)

    @blk.vector
    def _(v):
        b.wt(v)
        for tt in range(4):
            v.reduce_max(cols4[:, tt:tt + 1], psA[:, tt * 128:(tt + 1) * 128],
                         axis=AX.X)
        b.tick(v, b.sem_v)

    @blk.tensor
    def _(t):
        b.wv(t)
        ins = None
        for tt in range(4):
            ins = t.matmul(psB[0:1, tt * 128:(tt + 1) * 128], cols4[:, tt:tt + 1],
                           cst["c_id"], start=True, stop=True)
        b.mmtick(ins)

    @blk.vector
    def _(v):
        b.wt(v)
        v.tensor_copy(b.nrow_a[0:1, 0:512], psB[0:1, 0:512])
        b.tick(v, b.sem_v)

    @blk.tensor
    def _(t):
        b.wv(t)
        ins = t.matmul(psA[:, 0:512], cst["c_ones1x"][0:1, :],
                       b.nrow_a[0:1, 0:512], start=True, stop=True)
        b.mmtick(ins)

    @blk.vector
    def _(v):
        b.wt(v)
        v.tensor_copy(dst, psA[:, 0:512])
        b.tick(v, b.sem_v)


def _quant7(b, v, w, t1, t2, out, lo, hi, scale_tile=None, scale_col=None):
    if scale_tile is not None:
        v.tensor_mul(w, w, scale_tile)
    else:
        v.tensor_scalar(w, w, scale_col, None, ALU.mult)
    v.drain()
    v.tensor_scalar(t1, w, MAGIC2, MAGIC2, ALU.add, ALU.subtract)
    v.tensor_scalar(t2, w, MAGIC3, MAGIC3, ALU.add, ALU.subtract)
    v.tensor_scalar(w, w, MAGIC1, MAGIC1, ALU.add, ALU.subtract)
    v.drain()
    v.tensor_add(w, w, t1)
    v.drain()
    v.tensor_sub(w, w, t2)
    v.drain()
    v.tensor_scalar(out, w, lo, hi, ALU.max, ALU.min)


def stage_l1_ffn(b, xf):
    nc = b.nc
    hb = b.hb
    cst = b.cst
    C = b.C
    pool = b.pool
    U = 512
    pa = nc.lookup_mloc(pool.tensor).addr
    xa = nc.lookup_mloc(xf).addr

    wr1 = nc.dram_tensor("wr1", [1024, 4096], BF16, kind="Internal")
    wr2 = nc.dram_tensor("wr2", [4096, 1024], BF16, kind="Internal")
    ybuf = nc.dram_tensor("ybuf", [4096, T], F32, kind="Internal")

    xq16 = nc.alloc_sbuf_tensor_at("xq16", [128, NCH * T], BF16, offset=xa)
    wq16s = nc.alloc_sbuf_tensor_at("wq16s", [128, 4096], BF16, offset=pa + 10 * 2048)
    wsl16 = [nc.alloc_sbuf_tensor_at("wsl16_%d" % i, [128, 1024], BF16,
                                     offset=pa + (32 + i) * 2048) for i in range(2)]
    yq16 = nc.alloc_sbuf_tensor_at("yq16", [128, 1024], BF16, offset=pa)
    w2sl16 = [nc.alloc_sbuf_tensor_at("w2sl16_%d" % i, [128, 1024], BF16,
                                      offset=pa + (3 + i) * 2048) for i in range(2)]

    z = types.SimpleNamespace()
    z.mu = pool[:, 0:8 * U]
    z.hbuf = pool[:, 8 * U:16 * U]
    z.ysc = pool[:, 16 * U:24 * U]
    z.mcy = pool[:, 24 * U:32 * U]
    z.m2 = z.ysc
    cols = pool[:, 34 * U:35 * U]
    z.m1col = cols[:, 0:1]
    z.m2col = cols[:, 1:2]
    z.s1col = cols[:, 2:3]
    z.s2col = cols[:, 3:4]
    z.tmpcol = cols[:, 4:5]
    z.acc1 = cols[:, 5:6]
    z.acc2 = cols[:, 6:7]
    z.cw = cols[:, 8:104]
    z.cb = cols[:, 104:136]
    cells = pool[0:1, 34 * U + 136:34 * U + 144]
    z.m1cell = cells[:, 0:1]
    z.s1cell = cells[:, 1:2]
    z.m2cell = cells[:, 2:3]
    z.s2cell = cells[:, 3:4]
    z.vt = pool[:, 26 * U:34 * U]
    z.ps_s0 = b.pss[0][:, :]
    z.ps_s1 = b.pss[1][:, :]
    z.ps_v0 = b.pss[4][:, :]
    z.ps_v1 = b.pss[5][:, :]
    z.nrow = b.nrow_a
    z.nrowb = b.nrow_b

    with nc.Block() as blk:
        @blk.sync
        def _(sy):
            b.wt(sy)
            b.wv(sy)
            b.ws(sy)
            b.dma(sy, z.cw, hb["L1_ffn_cwP"].ap())
            b.dma(sy, z.cb, hb["L1_ffn_cbP"].ap())
            b.wdma(sy)

        @blk.scalar
        def _(s):
            b.wt(s)
            b.wv(s)

        @blk.vector
        def _(v):
            b.wt(v)
            b.ws(v)

        @blk.gpsimd
        def _(gp):
            b.wt(gp)
            b.wv(gp)
            b.ws(gp)
            for cc in range(NCH):
                b.dma(gp, b.xsave.ap()[cc * 128:(cc + 1) * 128, :],
                      xf[:, cc * T:(cc + 1) * T], b.sem_g)
            b.wdma(gp, b.sem_g)

    _l0f_ln_inplace(b, xf, z)

    wch = pool[:, 0:8 * U]
    t1 = pool[:, 8 * U:9 * U]
    t2 = pool[:, 9 * U:10 * U]
    for wi, (wname, nch_, cols_, mcell, scell, acc, mcol, scol) in enumerate((
            ("L1_ffn_w1T", 8, 4096, z.m1cell, z.s1cell, z.acc1, z.m1col, z.s1col),
            ("L1_ffn_w2T", 32, 1024, z.m2cell, z.s2cell, z.acc2, z.m2col, z.s2col))):
        with nc.Block() as blk:
            for ch in range(nch_):
                @blk.sync
                def _(sy, ch=ch, wname=wname, cols_=cols_):
                    g = b.sg.get("wch", 0)
                    if g:
                        sy.wait_ge(b.sem_v, g)
                    b.dma(sy, wch[:, 0:cols_],
                          hb[wname].ap()[ch * 128:(ch + 1) * 128, :])

                @blk.vector
                def _(v, ch=ch, cols_=cols_, acc=acc):
                    b.wdma(v)
                    v.reduce_sum(z.tmpcol, wch[:, 0:cols_], axis=AX.X,
                                 apply_absolute_value=True)
                    v.drain()
                    if ch == 0:
                        v.tensor_copy(acc, z.tmpcol)
                    else:
                        v.tensor_add(acc, acc, z.tmpcol)
                    b.tick(v, b.sem_v)
                    b.sg["wch"] = b.C.get(b.sem_v)

            @blk.tensor
            def _(t, acc=acc):
                b.wv(t)
                ins = t.matmul(z.ps_s0[0:1, 0:1], cst["c_onescols"][:, 0:1], acc,
                               start=True, stop=True)
                b.mmtick(ins)

            @blk.vector
            def _(v, mcell=mcell, scell=scell):
                b.wt(v)
                v.tensor_scalar(mcell, z.ps_s0[0:1, 0:1], 1.0 / 4194304.0, 1e-5,
                                ALU.mult, ALU.max)
                v.drain()
                v.reciprocal(scell, mcell)
                b.tick(v, b.sem_v)

            @blk.tensor
            def _(t, mcell=mcell, scell=scell):
                b.wv(t)
                t.matmul(z.ps_s1[:, 0:1], cst["c_ones1x"][0:1, :], mcell,
                         start=True, stop=True)
                ins = t.matmul(z.ps_s1[:, 1:2], cst["c_ones1x"][0:1, :], scell,
                               start=True, stop=True)
                b.mmtick(ins)

            @blk.vector
            def _(v, mcol=mcol, scol=scol):
                b.wt(v)
                v.tensor_copy(mcol, z.ps_s1[:, 0:1])
                v.tensor_copy(scol, z.ps_s1[:, 1:2])
                b.tick(v, b.sem_v)

    for wi, (wname, nch_, cols_, scol, wrh) in enumerate((
            ("L1_ffn_w1T", 8, 4096, z.s1col, wr1),
            ("L1_ffn_w2T", 32, 1024, z.s2col, wr2))):
        with nc.Block() as blk:
            for ch in range(nch_):
                @blk.sync
                def _(sy, ch=ch, wname=wname, cols_=cols_):
                    g = b.sg.get("wch", 0)
                    if g:
                        sy.wait_ge(b.sem_v, g)
                    b.dma(sy, wch[:, 0:cols_],
                          hb[wname].ap()[ch * 128:(ch + 1) * 128, :])

                @blk.vector
                def _(v, ch=ch, cols_=cols_, scol=scol):
                    b.wdma(v)
                    b.wdma(v, b.sem_g)
                    for sub in range(cols_ // U):
                        _quant7(b, v, wch[:, sub * U:(sub + 1) * U], t1, t2,
                                wch[:, sub * U:(sub + 1) * U], -1.0, 1.0,
                                scale_col=scol)
                        v.drain()
                    v.tensor_copy(wq16s[:, 0:cols_], wch[:, 0:cols_])
                    b.tick(v, b.sem_v)
                    b.sg["wch"] = b.C.get(b.sem_v)

                @blk.gpsimd
                def _(gp, ch=ch, cols_=cols_, wrh=wrh):
                    b.wv(gp)
                    b.dma(gp, wrh.ap()[ch * 128:(ch + 1) * 128, :],
                          wq16s[:, 0:cols_], b.sem_g)

    mcx = pool[:, 8 * U:9 * U]
    sct = pool[:, 9 * U:10 * U]
    qt1 = pool[:, 10 * U:11 * U]
    qt2 = pool[:, 11 * U:12 * U]
    W = 512
    for w in range(NCH):
        t0 = w * W
        with nc.Block() as blk:
            @blk.vector
            def _(v, t0=t0):
                b.wt(v)
                for cc in range(NCH):
                    src = xf[:, cc * T + t0:cc * T + t0 + W]
                    if cc == 0:
                        v.scalar_tensor_tensor(mcx, src, -1.0, src, ALU.mult, ALU.max)
                    else:
                        v.scalar_tensor_tensor(mcx, src, -1.0, mcx, ALU.mult, ALU.max)
                        v.drain()
                        v.tensor_tensor(mcx, mcx, src, ALU.max)
                    v.drain()
                b.tick(v, b.sem_v)

            _pmax_bc(b, blk, mcx, sct, qt1[:, 0:4], cst)

            @blk.vector
            def _(v, t0=t0):
                v.tensor_scalar(sct, sct, 1e-5, None, ALU.max)
                v.drain()
                v.reciprocal(sct, sct)
                v.drain()
                v.tensor_scalar(sct, sct, 127.0, None, ALU.mult)
                v.drain()
                v.reciprocal(z.mu[:, t0:t0 + W], sct)
                v.drain()
                v.tensor_scalar(z.mu[:, t0:t0 + W], z.mu[:, t0:t0 + W],
                                z.m1col, None, ALU.mult)
                v.drain()
                for cc in range(NCH):
                    sl = xf[:, cc * T + t0:cc * T + t0 + W]
                    _quant7(b, v, sl, qt1, qt2, sl, -128.0, 127.0,
                            scale_tile=sct)
                    v.drain()
                b.tick(v, b.sem_v)

    cnv0 = pool[:, 13 * U:21 * U]
    with nc.Block() as blk:
        @blk.vector
        def _(v):
            v.tensor_copy(cnv0, xf[:, 0:T])
            v.drain()
            v.tensor_copy(xq16[:, 0:T], cnv0)
            v.drain()
            for cc in range(1, NCH):
                v.tensor_copy(xq16[:, cc * T:(cc + 1) * T], xf[:, cc * T:(cc + 1) * T])
                v.drain()
            b.tick(v, b.sem_v)

    for hc in range(32):
        bufp = hc % 2
        with nc.Block() as blk:
            @blk.sync
            def _(sy, hc=hc, bufp=bufp):
                g = b.sg.get(("w4s", bufp), 0)
                if g:
                    sy.wait_ge(b.sem_t, g)
                b.dma(sy, wsl16[bufp][:, :].rearrange("p (c n) -> p c n", n=128),
                      wslab_src(wr1.ap(), hc * 128, 128))

            for w in range(NCH):
                @blk.tensor
                def _(t, hc=hc, bufp=bufp, w=w):
                    if w == 0:
                        b.wdma(t)
                    b.wv(t)
                    ps = z.ps_s0 if w % 2 == 0 else z.ps_s1
                    ins = None
                    for cc in range(NCH):
                        ins = t.matmul(ps[:, 0:W],
                                       wsl16[bufp][:, cc * 128:(cc + 1) * 128],
                                       xq16[:, cc * T + w * W:cc * T + (w + 1) * W],
                                       start=(cc == 0), stop=(cc == NCH - 1))
                    b.mmtick(ins)
                    if w == NCH - 1:
                        b.sg[("w4s", bufp)] = b.C.get(b.sem_t)

                @blk.vector
                def _(v, w=w):
                    b.wt(v)
                    ps = z.ps_s0 if w % 2 == 0 else z.ps_s1
                    v.tensor_mul(z.hbuf[:, w * W:(w + 1) * W], ps[:, 0:W],
                                 z.mu[:, w * W:(w + 1) * W])
                    b.tick(v, b.sem_v)

            @blk.vector
            def _(v, hc=hc):
                b.wdma(v, b.sem_g)
                w1c = z.cw[:, hc * 3 + 1:hc * 3 + 2]
                w0c = z.cw[:, hc * 3:hc * 3 + 1]
                w2c = z.cw[:, hc * 3 + 2:hc * 3 + 3]
                cbc = z.cb[:, hc:hc + 1]
                v.tensor_scalar(z.ysc[:, 0:T], z.hbuf[:, 0:T], w1c, cbc,
                                ALU.mult, ALU.add)
                v.drain()
                v.scalar_tensor_tensor(z.ysc[:, 1:T], z.hbuf[:, 0:T - 1], w0c,
                                       z.ysc[:, 1:T], ALU.mult, ALU.add)
                v.drain()
                v.scalar_tensor_tensor(z.ysc[:, 0:T - 1], z.hbuf[:, 1:T], w2c,
                                       z.ysc[:, 0:T - 1], ALU.mult, ALU.add)
                b.tick(v, b.sem_v)

            @blk.scalar
            def _(s):
                b.wv(s)
                s.activation(z.ysc[:, 0:T], z.ysc[:, 0:T], AF.Silu)
                b.tick(s, b.sem_s)

            @blk.vector
            def _(v, hc=hc):
                b.ws(v)
                if hc == 0:
                    v.scalar_tensor_tensor(z.mcy[:, 0:T], z.ysc[:, 0:T], -1.0,
                                           z.ysc[:, 0:T], ALU.mult, ALU.max)
                else:
                    v.scalar_tensor_tensor(z.mcy[:, 0:T], z.ysc[:, 0:T], -1.0,
                                           z.mcy[:, 0:T], ALU.mult, ALU.max)
                    v.drain()
                    v.tensor_tensor(z.mcy[:, 0:T], z.mcy[:, 0:T], z.ysc[:, 0:T],
                                    ALU.max)
                b.tick(v, b.sem_v)

            @blk.gpsimd
            def _(gp, hc=hc):
                b.ws(gp)
                b.dma(gp, ybuf.ap()[hc * 128:(hc + 1) * 128, :], z.ysc[:, 0:T],
                      b.sem_g)

    with nc.Block() as blk:
        for w8 in range(NCH):
            _pmax_bc(b, blk, z.mcy[:, w8 * W:(w8 + 1) * W],
                     z.m2[:, w8 * W:(w8 + 1) * W], pool[:, 9 * U:9 * U + 4], cst)

    ystg = [pool[:, 1 * U:2 * U], pool[:, 2 * U:3 * U]]
    xstg = [pool[:, 5 * U:6 * U], pool[:, 6 * U:7 * U]]
    ostg = [pool[:, 7 * U:8 * U], pool[:, 8 * U:9 * U]]
    qt1b = pool[:, 9 * U:10 * U]
    qt2b = pool[:, 10 * U:11 * U]
    scw = pool[:, 24 * U:25 * U]
    muw = pool[:, 25 * U:26 * U]

    for w in range(NCH):
        t0 = w * W
        with nc.Block() as blk:
            @blk.vector
            def _(v, t0=t0, w=w):
                b.wt(v)
                v.tensor_scalar(scw, z.m2[:, t0:t0 + W], 1e-5, None, ALU.max)
                v.drain()
                v.reciprocal(scw, scw)
                v.drain()
                v.tensor_scalar(scw, scw, 127.0, None, ALU.mult)
                v.drain()
                v.reciprocal(muw, scw)
                v.drain()
                v.tensor_scalar(muw, muw, z.m2col, None, ALU.mult)
                b.tick(v, b.sem_v)

            for hc in range(32):
                bufp = hc % 2

                @blk.sync
                def _(sy, hc=hc, bufp=bufp, t0=t0):
                    g = b.sg.get(("ystg", bufp), 0)
                    if g:
                        sy.wait_ge(b.sem_v, g)
                    b.dma(sy, ystg[bufp],
                          ybuf.ap()[hc * 128:(hc + 1) * 128, t0:t0 + W])

                @blk.gpsimd
                def _(gp, hc=hc, bufp=bufp):
                    g = b.sg.get(("w2s", bufp), 0)
                    if g:
                        gp.wait_ge(b.sem_t, g)
                    b.dma(gp, w2sl16[bufp][:, :],
                          wr2.ap()[hc * 128:(hc + 1) * 128, :], b.sem_g)

                @blk.vector
                def _(v, hc=hc, bufp=bufp):
                    b.wdma(v)
                    g = b.sg.get(("yq", bufp), 0)
                    if g:
                        v.wait_ge(b.sem_t, g)
                    _quant7(b, v, ystg[bufp], qt1b, qt2b,
                            yq16[:, bufp * W:(bufp + 1) * W], -128.0, 127.0,
                            scale_tile=scw)
                    b.tick(v, b.sem_v)
                    b.sg[("ystg", bufp)] = b.C.get(b.sem_v)

                @blk.tensor
                def _(t, hc=hc, bufp=bufp):
                    b.wdma(t, b.sem_g)
                    b.wv(t)
                    ins = None
                    for oj in range(NCH):
                        ins = t.matmul(b.pss[oj][:, 0:W],
                                       w2sl16[bufp][:, oj * 128:(oj + 1) * 128],
                                       yq16[:, bufp * W:(bufp + 1) * W],
                                       start=(hc == 0), stop=(hc == 31))
                    b.mmtick(ins)
                    b.sg[("w2s", bufp)] = b.C.get(b.sem_t)
                    b.sg[("yq", bufp)] = b.C.get(b.sem_t)

            for oj in range(NCH):
                op = oj % 2

                @blk.sync
                def _(sy, oj=oj, op=op, t0=t0):
                    g = b.sg.get(("xstg", op), 0)
                    if g:
                        sy.wait_ge(b.sem_v, g)
                    b.dma(sy, xstg[op],
                          b.xsave.ap()[oj * 128:(oj + 1) * 128, t0:t0 + W])

                @blk.vector
                def _(v, oj=oj, op=op):
                    b.wt(v)
                    b.wdma(v)
                    b.wdma(v, b.sem_g)
                    v.tensor_mul(ostg[op], b.pss[oj][:, 0:W], muw)
                    v.drain()
                    v.tensor_add(ostg[op], ostg[op], xstg[op])
                    b.tick(v, b.sem_v)
                    b.sg[("xstg", op)] = b.C.get(b.sem_v)

                @blk.gpsimd
                def _(gp, oj=oj, op=op, t0=t0):
                    b.wv(gp)
                    b.dma(gp, b.out_h.ap()[oj * 128:(oj + 1) * 128, t0:t0 + W],
                          ostg[op], b.sem_g)

    with nc.Block() as blk:
        @blk.gpsimd
        def _(gp):
            b.wdma(gp, b.sem_g)
            b.wdma(gp)

    b.out_written = True



def build_program(shapes):
    import contextlib
    nc = bass.Bass("TRN2", target_bir_lowering=False)
    es = contextlib.ExitStack()
    b = B(nc, es)
    for name, shape in shapes.items():
        b.hb[name] = nc.dram_tensor(name, list(shape), F32, kind="ExternalInput")
    out_h = nc.dram_tensor("outT", [D, T], F32, kind="ExternalOutput")
    b.out_h = out_h

    xf = b.sb("xf", [128, NCH * T])
    b.pss = [b.psum("ps%d" % i, [128, 512]) for i in range(8)]
    b.pool = b.sb("pool", [128, 35 * 512])[:, :]
    b.xsave = nc.dram_tensor("xsave", [D, T], F32, kind="Internal")
    for nm in ("c_id", "c_onescols", "c_ones2", "c_ones1x", "c_sel2", "c_sello",
               "c_selhi", "c_negup", "c_neglo", "c_triuincl", "c_rot", "c_mcur", "c_mprev"):
        t_ = b.sb("cst_" + nm, list(shapes[nm]))
        b.cst[nm] = t_[:, :]

    b.out_written = False
    with nc.Block() as blk:
        @blk.sync
        def _(sy):
            for cc in range(NCH):
                b.dma(sy, xf[:, cc * T:(cc + 1) * T],
                      b.hb["xT"].ap()[cc * 128:(cc + 1) * 128, :])
            for nm in b.cst:
                b.dma(sy, b.cst[nm], b.hb[nm].ap())
            b.wdma(sy)

    dbg_fired = False
    try:
        if N_STAGES >= 1:
            stage_l0_attn(b, xf)
        if N_STAGES >= 2:
            stage_l0_ffn(b, xf)
        if N_STAGES >= 3:
            stage_l1_attn(b, xf)
        if N_STAGES >= 4:
            stage_l1_ffn(b, xf)
    except DbgStop:
        dbg_fired = True

    if dbg_fired:
        return nc

    if not b.out_written:
        with nc.Block() as blk:
            @blk.gpsimd
            def _(gp):
                b.wt(gp)
                b.wv(gp)
                b.ws(gp)
                for cc in range(NCH):
                    b.dma(gp, out_h.ap()[cc * 128:(cc + 1) * 128, :],
                          xf[:, cc * T:(cc + 1) * T], b.sem_g)
                b.wdma(gp, b.sem_g)

    return nc


def _prep_common(inputs):
    w = {}
    for k, v in inputs.items():
        if k == "L0_attn_x":
            continue
        w[k] = np.ascontiguousarray(np.asarray(v, dtype=np.float32))
    w["L0_attn_qcwP"] = np.ascontiguousarray(
        w["L0_attn_qcw"].reshape(8, 128, 4).transpose(1, 0, 2).reshape(128, 32))
    w["L0_attn_kcwP"] = np.ascontiguousarray(
        w["L0_attn_kcw"].reshape(8, 128, 4).transpose(1, 0, 2).reshape(128, 32))
    w["L0_attn_vcwP"] = np.ascontiguousarray(
        w["L0_attn_vcw"].reshape(8, 128, 4).transpose(1, 0, 2).reshape(128, 32))
    w["L0_attn_qcbP"] = np.ascontiguousarray(w["L0_attn_qcb"].reshape(8, 128).T)
    w["L0_attn_kcbP"] = np.ascontiguousarray(w["L0_attn_kcb"].reshape(8, 128).T)
    w["L0_attn_vcbP"] = np.ascontiguousarray(w["L0_attn_vcb"].reshape(8, 128).T)
    w["L0_attn_abP"] = np.ascontiguousarray(
        np.stack([w["L0_attn_ba"], w["L0_attn_bb"]], axis=1))
    w["L0_attn_rms_wP"] = np.ascontiguousarray(w["L0_attn_rms_w"].reshape(8, 128).T)
    w["L0_ffn_lngP"] = np.ascontiguousarray(
        w["L0_ffn_ln_g"].reshape(2, 8, 128).transpose(2, 0, 1).reshape(128, 16))
    w["L0_ffn_lnbP"] = np.ascontiguousarray(
        w["L0_ffn_ln_b"].reshape(2, 8, 128).transpose(2, 0, 1).reshape(128, 16))
    w["L0_ffn_b2P"] = np.ascontiguousarray(
        w["L0_ffn_b2"].reshape(2, 8, 128).transpose(2, 0, 1).reshape(128, 16))
    b1a = np.zeros((2, 22, 128), dtype=np.float32)
    b1g = np.zeros((2, 22, 128), dtype=np.float32)
    for l_ in range(2):
        b1a[l_].reshape(-1)[:2730] = w["L0_ffn_b1"][l_][:2730]
        b1g[l_].reshape(-1)[:2730] = w["L0_ffn_b1"][l_][2730:]
    w["L0_ffn_b1aP"] = np.ascontiguousarray(b1a.transpose(2, 0, 1).reshape(128, 44))
    w["L0_ffn_b1gP"] = np.ascontiguousarray(b1g.transpose(2, 0, 1).reshape(128, 44))
    w["L0_ffn_qsP"] = np.ascontiguousarray(
        np.concatenate([w["L0_ffn_q_scale"]] * 2, axis=1).T)
    w["L0_ffn_ksP"] = np.ascontiguousarray(
        np.concatenate([w["L0_ffn_k_scale"]] * 2, axis=1).T)
    w["L1_ffn_w1T"] = np.ascontiguousarray(w["L1_ffn_w1"].T)
    w["L1_ffn_w2T"] = np.ascontiguousarray(w["L1_ffn_w2"].T)
    w["L1_attn_bqC"] = np.ascontiguousarray(w["L1_attn_bq"].reshape(8, 128).T)
    w["L1_attn_bkC"] = np.ascontiguousarray(w["L1_attn_bk"].reshape(8, 128).T)
    w["L1_attn_boC"] = np.ascontiguousarray(w["L1_attn_bo"].reshape(8, 128).T)
    w["L1_attn_bvR"] = np.ascontiguousarray(w["L1_attn_bv"].reshape(1, 1024))
    cix = np.arange(64)[:, None] * 64
    tix = np.arange(4096)[None, :]
    w["c_l1mask"] = np.where(cix <= tix, 0.0, -1e30).astype(np.float32)
    w["L1_ffn_cwP"] = np.ascontiguousarray(
        w["L1_ffn_conv_w"].reshape(32, 128, 3).transpose(1, 0, 2).reshape(128, 96))
    w["L1_ffn_cbP"] = np.ascontiguousarray(w["L1_ffn_conv_b"].reshape(32, 128).T)
    w.update(host_consts())
    return w


def kernel(**inputs):
    common = _prep_common(inputs)
    x = np.asarray(inputs["L0_attn_x"], dtype=np.float32)
    in_maps = []
    for core in range(N_CORES):
        m = dict(common)
        m["xT"] = np.ascontiguousarray(x[core].T)
        in_maps.append(m)
    shapes = {k: list(v.shape) for k, v in in_maps[0].items()}
    nc = build_program(shapes)
    res = run_bass_kernel_spmd(nc, in_maps, core_ids=list(range(N_CORES)))
    out = np.stack([np.asarray(r["outT"]).T for r in res.results], axis=0)
    return np.ascontiguousarray(out)
